# version 19
# baseline (speedup 1.0000x reference)
import math
import contextlib
import numpy as np
import ml_dtypes
import concourse.bass as bass
import concourse.mybir as mybir
from concourse.bass_utils import run_bass_kernel_spmd

F32 = mybir.dt.float32
BF16 = mybir.dt.bfloat16
AF = mybir.ActivationFunctionType
ALU = mybir.AluOpType

NCORE = 8
NT = 8
TLOC = 1024
D = 4096
FC = 32
HQ = 32
HKV = 8
HD = 128
SM_SCALE = 1.0 / math.sqrt(HD)
LN_EPS = 1e-5



class Sched:

    ENGS = ("tensor", "vector", "scalar", "gpsimd", "sync")

    def __init__(self, nc):
        self.nc = nc
        self.handles = {}
        self.counts = {}
        self.ops = {e: [] for e in self.ENGS}
        self.waited = {e: {} for e in self.ENGS}

    def sem(self, name):
        h = self.nc.alloc_semaphore(name)
        self.handles[name] = h
        self.counts[name] = 0
        return h

    def op(self, eng, fn, waits=(), inc=None, inc_amt=1):
        w = {}
        for item in waits:
            if item is None:
                continue
            s, v = item
            if v is None or v <= 0:
                continue
            if self.waited[eng].get(s, 0) >= v:
                continue
            w[s] = max(w.get(s, 0), v)
        for s, v in w.items():
            self.waited[eng][s] = v
        val = None
        if inc is not None:
            self.counts[inc] += inc_amt
            val = self.counts[inc]
            self.waited[eng][inc] = max(self.waited[eng].get(inc, 0), val)
        self.ops[eng].append((tuple(w.items()), fn, inc, inc_amt))
        return (inc, val) if inc is not None else None

    def dma(self, eng, out, in_, sem, waits=(), **kw):
        return self.op(
            eng,
            lambda e, out=out, in_=in_, kw=kw: e.dma_start(out=out, in_=in_, **kw),
            waits=waits, inc=sem, inc_amt=16)

    def replay(self, eng_name, eng):
        for waits, fn, inc, inc_amt in self.ops[eng_name]:
            for s, v in waits:
                eng.wait_ge(self.handles[s], v)
            ins = fn(eng)
            if inc is not None:
                ins.then_inc(self.handles[inc], inc_amt)


def build_nc(detect_races=True):
    nc = bass.Bass("TRN2", target_bir_lowering=False, dynamic_dma_scratch_size=8192,
                   detect_race_conditions=detect_races)
    OFF_A = (nc.sbuf_base + 63) & ~63
    OFF_B = OFF_A + 65536
    OFF_C = OFF_A + 131072
    OFF_TAB = OFF_A + 196608
    OFF_G = OFF_A + 204800

    x_in = nc.dram_tensor("x", [TLOC, D], F32, kind="ExternalInput")
    wqs = nc.dram_tensor("wqs", [512, D], F32, kind="ExternalInput")
    wks = nc.dram_tensor("wks", [512, 1024], F32, kind="ExternalInput")
    wvs = nc.dram_tensor("wvs", [512, 1024], F32, kind="ExternalInput")
    wos = nc.dram_tensor("wos", [512, D], F32, kind="ExternalInput")
    w12s = nc.dram_tensor("w12s", [1024, D], F32, kind="ExternalInput")
    w3s = nc.dram_tensor("w3s", [512, D], F32, kind="ExternalInput")
    b12 = nc.dram_tensor("b12", [8192], F32, kind="ExternalInput")
    b3 = nc.dram_tensor("b3", [D], F32, kind="ExternalInput")
    ropec = nc.dram_tensor("ropec", [128, TLOC], F32, kind="ExternalInput")
    ropes = nc.dram_tensor("ropes", [128, TLOC], F32, kind="ExternalInput")
    masks = nc.dram_tensor("masks", [8, 128, 512], BF16, kind="ExternalInput")
    out = nc.dram_tensor("out", [TLOC, D], F32, kind="ExternalOutput")

    wq_c = nc.dram_tensor("wq_c", [16384, 128], BF16, kind="Internal")
    wk_c = nc.dram_tensor("wk_c", [4096, 128], BF16, kind="Internal")
    wv_c = nc.dram_tensor("wv_c", [512, 1024], BF16, kind="Internal")
    wo_c = nc.dram_tensor("wo_c", [512, D], BF16, kind="Internal")
    wq_g = nc.dram_tensor("wq_g", [NCORE * 16384, 128], BF16, kind="Internal", addr_space="Shared")
    wk_g = nc.dram_tensor("wk_g", [NCORE * 4096, 128], BF16, kind="Internal", addr_space="Shared")
    wv_g = nc.dram_tensor("wv_g", [D, 1024], BF16, kind="Internal", addr_space="Shared")
    wo_g = nc.dram_tensor("wo_g", [D, D], BF16, kind="Internal", addr_space="Shared")
    w12t_c = nc.dram_tensor("w12t_c", [D, 1024], BF16, kind="Internal")
    w3t_c = nc.dram_tensor("w3t_c", [D, 512], BF16, kind="Internal")
    k_c = nc.dram_tensor("k_c", [1024, 1024], BF16, kind="Internal")
    v_c = nc.dram_tensor("v_c", [1024, 1024], BF16, kind="Internal")
    w12t_g = nc.dram_tensor("w12t_g", [NCORE * D, 1024], BF16, kind="Internal", addr_space="Shared")
    w3t_g = nc.dram_tensor("w3t_g", [NCORE * D, 512], BF16, kind="Internal", addr_space="Shared")
    k_g = nc.dram_tensor("k_g", [8192, 1024], BF16, kind="Internal", addr_space="Shared")
    v_g = nc.dram_tensor("v_g", [8192, 1024], BF16, kind="Internal", addr_space="Shared")
    xres_d = nc.dram_tensor("xres_d", [TLOC, D], F32, kind="Internal")

    id128f = nc.inline_tensor(np.eye(128, dtype=np.float32), "id128f")
    psw = np.zeros((128, 128), dtype=ml_dtypes.bfloat16)
    psw[np.arange(128) ^ 1, np.arange(128)] = 1
    pswap_c = nc.inline_tensor(psw, "pswap_c")
    ones_col_c = nc.inline_tensor(np.ones((128, 1), dtype=ml_dtypes.bfloat16), "ones_col_c")
    ones_row_c = nc.inline_tensor(np.ones((1, 128), dtype=ml_dtypes.bfloat16), "ones_row_c")
    ones_rowf_c = nc.inline_tensor(np.ones((1, 128), dtype=np.float32), "ones_rowf_c")

    SA = nc.alloc_sbuf_tensor_at
    a_xlnT = SA("a_xlnT", [128, FC, 1024], BF16, offset=OFF_A)
    a_kb = [SA(f"a_kb{i}", [128, 64, 128], BF16, offset=OFF_A + i * 32768) for i in range(2)]
    a_vb = [SA(f"a_vb{i}", [128, 64, 128], BF16, offset=OFF_A + i * 32768 + 16384) for i in range(2)]
    a_wo = [SA(f"a_wo{i}", [128, 32, 512], BF16, offset=OFF_A + i * 32768) for i in range(2)]
    a_h2T = SA("a_h2T", [128, FC, 1024], BF16, offset=OFF_A)

    b_wv = SA("b_wv", [128, FC, 512], BF16, offset=OFF_B)
    b_ktl = SA("b_ktl", [128, HKV, 1024], BF16, offset=OFF_B + 32768)
    b_vtl = SA("b_vtl", [128, NT, 1024], BF16, offset=OFF_B + 49152)
    b_qT = SA("b_qT", [128, NT, HQ, 128], BF16, offset=OFF_B)
    b_w12 = SA("b_w12", [128, FC, 1024], BF16, offset=OFF_B)
    b_w3 = [SA(f"b_w3{i}", [128, 32, 512], BF16, offset=OFF_B + i * 32768) for i in range(2)]

    c_stage = SA("c_stage", [128, D], F32, offset=OFF_C)
    c_cast = SA("c_cast", [128, D], BF16, offset=OFF_C + 16384)
    c_band = [SA(f"c_band{i}", [128, 8, 128], BF16, offset=OFF_C + 24576 + i * 2048) for i in range(2)]
    c_x = SA("c_x", [128, D], F32, offset=OFF_C + 28672)
    c_wbuf = [SA(f"c_wbuf{i}", [128, FC, 128], BF16, offset=OFF_C + 45056 + i * 8192) for i in range(2)]
    c_oT = SA("c_oT", [128, NT, HQ, 128], BF16, offset=OFF_C)
    c_xres = SA("c_xres", [128, D], F32, offset=OFF_C + 28672)
    c_x4 = SA("c_x4", [128, 32, 1024], BF16, offset=OFF_C)

    t_ropec = SA("t_ropec", [128, 1024], F32, offset=OFF_TAB)
    t_ropes = SA("t_ropes", [128, 1024], F32, offset=OFF_TAB + 4096)
    t_masks = SA("t_masks", [128, 8, 512], BF16, offset=OFF_TAB)
    t_b3 = SA("t_b3", [1, D], BF16, offset=OFF_TAB)

    gofs = OFF_G
    def galloc(name, shape, dtype):
        nonlocal gofs
        h = SA(name, shape, dtype, offset=gofs)
        nbytes = int(np.prod(shape[1:])) * mybir.dt.size(dtype)
        gofs += (nbytes + 63) & ~63
        return h
    g_id = galloc("g_id", [128, 128], F32)
    g_pswap = galloc("g_pswap", [128, 128], BF16)
    g_onesc = galloc("g_onesc", [128, 1], BF16)
    g_onesr = galloc("g_onesr", [1, 128], BF16)
    g_onesrf = galloc("g_onesrf", [1, 128], F32)
    g_b12 = galloc("g_b12", [128, 64], F32)
    g_b12st = galloc("g_b12st", [64, 128], F32)
    g_eps = galloc("g_eps", [128, 1], F32)
    g_stats = galloc("g_stats", [128, 48], F32)
    g_mv = galloc("g_mv", [128, 2], F32)
    g_t1 = galloc("g_t1", [128, 1], F32)
    g_rstd = galloc("g_rstd", [128, 1], F32)
    govl = gofs
    g_kqbf = galloc("g_kqbf", [128, 512], BF16)
    g_sc1 = galloc("g_sc1", [128, 512], F32)
    g_sc2 = galloc("g_sc2", [128, 512], F32)
    gofs = govl
    g_pt = [galloc(f"g_pt{i}", [128, 1024], BF16) for i in range(2)]
    g_ptm = [galloc(f"g_ptm{i}", [128, 1024], BF16) for i in range(2)]
    g_rl = galloc("g_rl", [1, 512], F32)
    g_rlb = galloc("g_rlb", [128, 512], F32)
    gend = gofs
    gofs = govl
    g_x2s = [galloc(f"g_x2s{i}", [128, 512], BF16) for i in range(2)]
    g_xin = [galloc(f"g_xin{i}", [128, 512], F32) for i in range(2)]
    g_xrs = [galloc(f"g_xrs{i}", [128, 512], F32) for i in range(2)]
    gend = max(gend, gofs)
    assert gend <= nc.sbuf_top, f"G overflow: {gend} > {nc.sbuf_top}"

    ctx = contextlib.ExitStack()
    PSA = ctx.enter_context(nc.psum_tensor("PSA", [128, 1024], F32))
    PSB = ctx.enter_context(nc.psum_tensor("PSB", [128, 1024], F32))
    PSC = ctx.enter_context(nc.psum_tensor("PSC", [128, 1024], F32))
    PSD = ctx.enter_context(nc.psum_tensor("PSD", [128, 1024], F32))
    ACCS = [PSA[:, 0:512], PSA[:, 512:1024], PSB[:, 0:512], PSB[:, 512:1024]]
    ACCS6 = ACCS + [PSC[:, 0:512], PSC[:, 512:1024]]
    O_PS = [PSC[:, 0:512], PSD[:, 0:512]]
    L_PS = [PSC[0:1, 512:1024], PSD[0:1, 512:1024]]
    BC_PS = [PSC[:, 512:1024], PSD[:, 512:1024]]
    TRB = [PSD, PSA]

    sch = Sched(nc)
    for s in ("dx", "dwk0", "dwk1", "dwq0", "dwq1", "dwv", "dwo0", "dwo1",
              "dw12g0", "dw12g1", "dw12g2", "dw12g3", "dw3g0", "dw3g1", "dkv0", "dkv1",
              "dxin0", "dxin1", "dxr", "dxb", "dxr3d0", "dxr3d1", "dout",
              "dtab", "dmsk", "dstg", "dcb", "dck", "dcv",
              "dcwk", "dcwv", "dcwq", "dcwo", "db3",
              "ccwk", "ccwv", "ccwq", "ccwo", "ccw12", "ccw3", "cck", "ccv",
              "pe", "act", "dve"):
        sch.sem(s)

    def all_gather(in_t, out_t, ccname, waits):
        return sch.op(
            "gpsimd",
            lambda e, in_t=in_t, out_t=out_t: e.collective_compute(
                "AllGather", ALU.bypass, replica_groups=[list(range(NCORE))],
                ins=[in_t.ap().opt()], outs=[out_t.ap().opt()]),
            waits=waits, inc=ccname)

    sch.dma("sync", g_id[:], id128f.ap(), "dtab")
    sch.dma("sync", g_pswap[:], pswap_c.ap(), "dtab")
    sch.dma("sync", g_onesc[:], ones_col_c.ap(), "dtab")
    sch.dma("sync", g_onesr[:], ones_row_c.ap(), "dtab")
    sch.dma("sync", g_onesrf[:], ones_rowf_c.ap(), "dtab")
    sch.dma("sync", t_ropec[:], ropec[:, :], "dtab")
    sch.dma("sync", t_ropes[:], ropes[:, :], "dtab")
    w_consts = sch.dma("sync", g_b12st[:], b12.ap().rearrange("(j p) -> j p", p=128), "dtab")
    sch.op("vector", lambda e: e.memset(g_eps[:], LN_EPS), inc="dve")
    w_b12tr = sch.op("tensor",
                     lambda e: e.transpose(PSC[:, 0:64], g_b12st[:], g_id[0:64, 0:64]),
                     waits=[w_consts], inc="pe")
    w_b12 = sch.op("vector", lambda e: e.tensor_copy(g_b12[:], PSC[:, 0:64]),
                   waits=[w_b12tr], inc="dve")

    stage_state = {"free": None, "cast_free": None}

    def cast_gather(Wsrc, cols, Wc, Wg, ccname, blocked_mc=None, defer=False):
        csem = "dc" + ccname[2:]
        w_out = None
        for t in range(4):
            w_st = sch.dma("sync", c_stage[:, 0:cols],
                           Wsrc[t * 128:(t + 1) * 128, :], "dstg",
                           waits=[stage_state["free"]])
            w_cst = sch.op(
                "scalar",
                lambda e, cols=cols: e.copy(c_cast[:, 0:cols], c_stage[:, 0:cols]),
                waits=[w_st, stage_state["cast_free"]], inc="act")
            stage_state["free"] = w_cst
            if blocked_mc is None:
                dst = Wc[t * 128:(t + 1) * 128, :]
                src = c_cast[:, 0:cols]
            else:
                dst = Wc.ap().rearrange("(mc p f) d -> f p mc d",
                                        mc=blocked_mc, p=128, f=4)[t]
                src = c_cast[:, 0:cols].rearrange("p (mc d) -> p mc d", mc=blocked_mc)
            w_out = sch.dma("sync", dst, src, csem, waits=[w_cst])
            stage_state["cast_free"] = w_out
        if defer:
            return w_out
        return all_gather(Wc, Wg, ccname, [w_out])

    cast_gather(wks, 1024, wk_c, wk_g, "ccwk", blocked_mc=8)
    cast_gather(wvs, 1024, wv_c, wv_g, "ccwv")
    cast_gather(wqs, D, wq_c, wq_g, "ccwq", blocked_mc=32)

    w_lnx_pe = None
    evac_free = {0: None, 1: None}
    w_xlnT = None
    for tt in range(NT):
        w_x = sch.dma("sync", c_x[:], x_in[tt * 128:(tt + 1) * 128, :], "dx",
                      waits=[w_lnx_pe])
        w_bs = None
        for i in range(8):
            w_bs = sch.op(
                "vector",
                lambda e, i=i: e.bn_stats(g_stats[:, i * 6:(i + 1) * 6],
                                          c_x[:, i * 512:(i + 1) * 512]),
                waits=[w_x] if i == 0 else [], inc="dve" if i == 7 else None)
        w_ag = sch.op("vector", lambda e: e.bn_aggr(g_mv[:], g_stats[:]),
                      waits=[w_bs], inc="dve")
        w_ln = sch.op("scalar",
                      lambda e: e.activation(g_t1[:], g_mv[:, 1:2], AF.Ln, bias=g_eps[:]),
                      waits=[w_ag], inc="act")
        w_rs = sch.op("scalar",
                      lambda e: e.activation(g_rstd[:], g_t1[:], AF.Exp, scale=-0.5),
                      waits=[w_ln], inc="act")
        w_ap = sch.op(
            "vector",
            lambda e: e.tensor_scalar(c_x[:], c_x[:], g_mv[:, 0:1], g_rstd[:],
                                      ALU.subtract, ALU.mult),
            waits=[w_rs], inc="dve")
        for kb in range(4):
            par = kb % 2
            trb = TRB[par]
            last_tr = None
            for kc8 in range(8):
                kc = kb * 8 + kc8
                last_tr = sch.op(
                    "tensor",
                    lambda e, trb=trb, kc8=kc8, kc=kc: e.transpose(
                        trb[:, kc8 * 128:(kc8 + 1) * 128],
                        c_x[:, kc * 128:(kc + 1) * 128], g_id[:]),
                    waits=[w_ap, evac_free[par]] if kc8 == 0 else [],
                    inc="pe" if kc8 == 7 else None)
            w_ev = sch.op(
                "scalar",
                lambda e, trb=trb, kb=kb, tt=tt: e.copy(
                    a_xlnT[:, kb * 8:(kb + 1) * 8, tt * 128:(tt + 1) * 128],
                    trb[:].rearrange("p (a q) -> p a q", a=8)),
                waits=[last_tr], inc="act")
            evac_free[par] = w_ev
        w_lnx_pe = last_tr
        w_xlnT = w_ev

    acc_free = [None, None, None, None]
    swap_free = w_b12
    kq_free = None
    wbuf_free = {0: None, 1: None}
    state = {"slot": 0, "lastpe": None}

    def proj_rope(n_mc, wsrc_g, wsem, ccname, dst, extra_first):
        nonlocal swap_free, kq_free
        first = True
        for mc in range(n_mc):
            par = mc % 2
            src_r = wsrc_g.ap().rearrange("(r mc p f) d -> mc p r f d",
                                          r=8, mc=n_mc, p=128, f=4)[mc]
            dst_r = c_wbuf[par][:].rearrange("p (r f) d -> p r f d", r=8)
            psem = wsem + str(par)
            sch.dma("sync", dst_r[:, 0:4], src_r[:, 0:4], psem,
                    waits=[wbuf_free[par], (ccname, 1) if mc < 2 else None])
            w_w = sch.dma("sync", dst_r[:, 4:8], src_r[:, 4:8], psem)
            slA = state["slot"] % 4
            slB = (state["slot"] + 1) % 4
            state["slot"] += 2
            accA, accB = ACCS[slA], ACCS[slB]
            fw = [w_w, acc_free[slA], acc_free[slB], w_xlnT]
            if first:
                fw = fw + extra_first
                first = False
            last = None
            for fc in range(FC):
                sch.op(
                    "tensor",
                    lambda e, acc=accA, par=par, fc=fc: e.matmul(
                        acc, c_wbuf[par][:, fc, :], a_xlnT[:, fc, 0:512],
                        start=(fc == 0), stop=(fc == FC - 1)),
                    waits=fw if fc == 0 else [])
                last = sch.op(
                    "tensor",
                    lambda e, acc=accB, par=par, fc=fc: e.matmul(
                        acc, c_wbuf[par][:, fc, :], a_xlnT[:, fc, 512:1024],
                        start=(fc == 0), stop=(fc == FC - 1)),
                    inc="pe" if fc == FC - 1 else None)
            wbuf_free[par] = last
            state["lastpe"] = last
            for th, acc, sl in ((0, accA, slA), (1, accB, slB)):
                w_evv = sch.op("scalar", lambda e, acc=acc: e.copy(g_kqbf[:], acc),
                               waits=[last, kq_free], inc="act")
                w_sw = sch.op(
                    "tensor",
                    lambda e: e.matmul(PSC[:, 0:512], g_pswap[:], g_kqbf[:],
                                       start=True, stop=True),
                    waits=[w_evv, swap_free], inc="pe")
                kq_free = w_sw
                w_t1 = sch.op(
                    "vector",
                    lambda e, th=th: e.tensor_tensor(
                        g_sc1[:], PSC[:, 0:512],
                        t_ropes[:, th * 512:(th + 1) * 512], ALU.mult),
                    waits=[w_sw], inc="dve")
                swap_free = w_t1
                w_t2 = sch.op(
                    "vector",
                    lambda e, acc=acc, th=th: e.tensor_tensor(
                        g_sc2[:], acc, t_ropec[:, th * 512:(th + 1) * 512], ALU.mult),
                    inc="dve")
                acc_free[sl] = w_t2
                w_t3 = sch.op(
                    "vector",
                    lambda e, mc=mc, th=th, dst=dst: e.tensor_tensor(
                        (dst[:, mc, th * 512:(th + 1) * 512] if len(dst.shape) == 3
                         else dst[:, th * 4:(th + 1) * 4, mc, :]),
                        g_sc1[:], g_sc2[:], ALU.add),
                    waits=[w_t1, w_t2], inc="dve")
        return w_t3

    w_klast = proj_rope(HKV, wk_g, "dwk", "ccwk", b_ktl, [])


    tr_free = {0: None, 1: None}
    band_free = {0: None, 1: None}
    w_con = None
    for (src, dst, ntile) in ((w12s, w12t_c, 8), (w3s, w3t_c, 4)):
        for mj in range(ntile):
            w_st = sch.dma("sync", c_stage[:], src[mj * 128:(mj + 1) * 128, :], "dstg",
                           waits=[stage_state["free"]])
            for kb in range(4):
                par = kb % 2
                trb = TRB[par]
                last_tr = None
                for kc8 in range(8):
                    kc = kb * 8 + kc8
                    last_tr = sch.op(
                        "tensor",
                        lambda e, trb=trb, kc8=kc8, kc=kc: e.transpose(
                            trb[:, kc8 * 128:(kc8 + 1) * 128],
                            c_stage[:, kc * 128:(kc + 1) * 128], g_id[:]),
                        waits=[w_st, tr_free[par], evac_free[0], evac_free[1],
                               acc_free[0], acc_free[1]]
                        if kc8 == 0 else [],
                        inc="pe" if kc8 == 7 else None)
                w_ev = sch.op(
                    "scalar",
                    lambda e, trb=trb, par=par: e.copy(
                        c_band[par][:], trb[:].rearrange("p (a q) -> p a q", a=8)),
                    waits=[last_tr, band_free[par]], inc="act")
                tr_free[par] = w_ev
                for half in range(2):
                    w_con = sch.dma(
                        "sync",
                        dst.ap()[kb * 1024 + half * 512:kb * 1024 + (half + 1) * 512,
                                 mj * 128:(mj + 1) * 128].rearrange(
                            "(kc p) m -> p kc m", p=128),
                        c_band[par][:, half * 4:(half + 1) * 4, :], "dcb",
                        waits=[w_ev])
                band_free[par] = w_con
            stage_state["free"] = last_tr

    w_vlast = None
    for dh in range(2):
        src_r = wv_g.ap().rearrange("(fc p) d -> p fc d", p=128)[
            :, :, dh * 512:(dh + 1) * 512]
        sch.dma("sync", b_wv[:, 0:16, :], src_r[:, 0:16, :], "dwv",
                waits=[("ccwv", 1) if dh == 0 else None,
                       state["lastpe"] if dh == 1 else None])
        w_wv = sch.dma("sync", b_wv[:, 16:32, :], src_r[:, 16:32, :], "dwv")
        for tt in range(NT):
            sl = state["slot"] % 4
            state["slot"] += 1
            acc = ACCS[sl]
            last = None
            for fc in range(FC):
                last = sch.op(
                    "tensor",
                    lambda e, acc=acc, fc=fc, tt=tt: e.matmul(
                        acc, a_xlnT[:, fc, tt * 128:(tt + 1) * 128], b_wv[:, fc, 0:512],
                        start=(fc == 0), stop=(fc == FC - 1)),
                    waits=[w_wv, acc_free[sl], tr_free[0], tr_free[1]]
                    if fc == 0 else [],
                    inc="pe" if fc == FC - 1 else None)
            state["lastpe"] = last
            w_vlast = sch.op(
                "scalar",
                lambda e, acc=acc, tt=tt, dh=dh: e.copy(
                    b_vtl[:, tt, dh * 512:(dh + 1) * 512], acc),
                waits=[last], inc="act")
            acc_free[sl] = w_vlast


    w_conk = None
    for hk in range(HKV):
        w_conk = sch.dma("sync", k_c.ap()[hk * 128:(hk + 1) * 128, :],
                         b_ktl[:, hk, :], "dck", waits=[w_klast])
    all_gather(k_c, k_g, "cck", [w_conk])
    w_conv = None
    for tt in range(NT):
        w_conv = sch.dma("sync", v_c.ap()[tt * 128:(tt + 1) * 128, :],
                         b_vtl[:, tt, :], "dcv", waits=[w_vlast])
    all_gather(v_c, v_g, "ccv", [w_conv])

    w_qlast = proj_rope(HQ, wq_g, "dwq", "ccwq", b_qT, [w_conk, w_conv])

    all_gather(w12t_c, w12t_g, "ccw12", [w_con])
    all_gather(w3t_c, w3t_g, "ccw3", [])
    cast_gather(wos, D, wo_c, wo_g, "ccwo")

    w_masks = sch.dma("sync", t_masks[:],
                      masks.ap().rearrange("j k q -> k j q"), "dmsk",
                      waits=[w_qlast])
    kg_r = k_g.ap().rearrange("(r h p) t -> h p r t", r=8, h=8)
    vg_r = v_g.ap().rearrange("(r j p) c -> j p r c", r=8, j=8)

    qk_val = {}
    exp_val = {}
    lav_val = {}
    norm_done = {}
    bcast_val = {}
    kv_val = {}
    kvbuf_free = {0: None, 1: None}
    gpair = 0
    gslot = 0
    pending = None

    def emit_lav(gp, info):
        hk, slot, p, par, par_o, masked, P = info
        src = g_ptm[gp % 2] if masked else g_pt[gp % 2]
        w_src = exp_val[gp] if not masked else mask_val[gp]
        lastw = None
        for cidx in range(2):
            st = (p == 0) and cidx == 0
            sp = (p == P - 1) and cidx == 1
            sch.op(
                "tensor",
                lambda e, par_o=par_o, src=src, cidx=cidx, st=st, sp=sp: e.matmul(
                    L_PS[par_o], g_onesc[:], src[:, cidx * 512:(cidx + 1) * 512],
                    start=st, stop=sp),
                waits=([w_src] + ([norm_done.get(gslot_of[gp] - 2)] if st else []))
                if cidx == 0 else [])
        for cidx in range(2):
            kt = 2 * p + cidx
            st = (p == 0) and cidx == 0
            sp = (p == P - 1) and cidx == 1
            lastw = sch.op(
                "tensor",
                lambda e, par=par, par_o=par_o, src=src, kt=kt, st=st, sp=sp: e.matmul(
                    O_PS[par_o], a_vb[par][:, kt, :], src[:, (kt % 2) * 512:(kt % 2 + 1) * 512],
                    start=st, stop=sp),
                inc="pe" if cidx == 1 else None)
        lav_val[gp] = lastw
        return lastw

    mask_val = {}
    gslot_of = {}
    lav_queue = []

    for hk in range(HKV):
        par = hk % 2
        ksem = "dkv" + str(par)
        for j in range(8):
            w1 = sch.dma("sync", a_kb[par][:, 8 * j:8 * j + 8, :],
                         kg_r[hk][:, :, j * 128:(j + 1) * 128], ksem,
                         waits=[("cck", 1) if hk == 0 and j == 0 else None,
                                ("ccv", 1) if hk == 0 and j == 0 else None,
                                kvbuf_free[par]])
            w2 = sch.dma("sync", a_vb[par][:, 8 * j:8 * j + 8, :],
                         vg_r[j][:, :, hk * 128:(hk + 1) * 128], ksem)
        kv_full = w2
        for slot in range(8):
            P = 4 * slot + 4
            par_o = gslot % 2
            for p in range(P):
                pbuf = gpair % 2
                PS_pair = PSA if pbuf == 0 else PSB
                gslot_of[gpair] = gslot
                masked = p >= P - 4
                sch.op(
                    "tensor",
                    lambda e, PS_pair=PS_pair, par=par, p=p, hk=hk, slot=slot: e.matmul(
                        PS_pair[:, 0:512], a_kb[par][:, 2 * p, :],
                        b_qT[:, slot, hk * 4:(hk + 1) * 4, :],
                        start=True, stop=True),
                    waits=[exp_val.get(gpair - 2), kv_full,
                           w_qlast if gpair == 0 else None])
                w_qk = sch.op(
                    "tensor",
                    lambda e, PS_pair=PS_pair, par=par, p=p, hk=hk, slot=slot: e.matmul(
                        PS_pair[:, 512:1024], a_kb[par][:, 2 * p + 1, :],
                        b_qT[:, slot, hk * 4:(hk + 1) * 4, :],
                        start=True, stop=True),
                    inc="pe")
                qk_val[gpair] = w_qk
                if p == 0 and pending is not None:
                    pgs, ppar_o, w_recip = pending
                    w_bc = sch.op(
                        "tensor",
                        lambda e, ppar_o=ppar_o: e.matmul(
                            BC_PS[ppar_o], g_onesrf[:], g_rl[:], start=True, stop=True),
                        waits=[w_recip], inc="pe")
                    bcast_val[pgs] = w_bc
                    w_evb = sch.op("vector", lambda e, ppar_o=ppar_o: e.tensor_copy(
                        g_rlb[:], BC_PS[ppar_o]), waits=[w_bc], inc="dve")
                    phk, pslot = divmod(pgs, 8)
                    w_mu = sch.op(
                        "vector",
                        lambda e, ppar_o=ppar_o, phk=phk, pslot=pslot: e.tensor_tensor(
                            c_oT[:, pslot, phk * 4:(phk + 1) * 4, :],
                            O_PS[ppar_o], g_rlb[:], ALU.mult),
                        waits=[w_evb] + ([stage_state["free"], stage_state["cast_free"]]
                                         if pgs == 0 else []),
                        inc="dve")
                    norm_done[pgs] = w_mu
                    pending = None
                if len(lav_queue) >= 2:
                    emit_lav(*lav_queue.pop(0))
                w_exp = sch.op(
                    "scalar",
                    lambda e, PS_pair=PS_pair, pbuf=pbuf: e.activation(
                        g_pt[pbuf][:], PS_pair[:], AF.Exp, scale=SM_SCALE),
                    waits=[w_qk, lav_val.get(gpair - 2), w_masks if gpair == 0 else None],
                    inc="act")
                exp_val[gpair] = w_exp
                if masked:
                    jj0 = 2 * p - 8 * slot
                    sch.op(
                        "vector",
                        lambda e, pbuf=pbuf, jj0=jj0: e.tensor_tensor(
                            g_ptm[pbuf][:, 0:512], g_pt[pbuf][:, 0:512],
                            t_masks[:, jj0, :], ALU.mult),
                        waits=[w_exp, lav_val.get(gpair - 2)])
                    w_m = sch.op(
                        "vector",
                        lambda e, pbuf=pbuf, jj0=jj0: e.tensor_tensor(
                            g_ptm[pbuf][:, 512:1024], g_pt[pbuf][:, 512:1024],
                            t_masks[:, jj0 + 1, :], ALU.mult),
                        inc="dve")
                    mask_val[gpair] = w_m
                lav_queue.append((gpair, (hk, slot, p, par, par_o, masked, P)))
                gpair += 1
            while lav_queue:
                emit_lav(*lav_queue.pop(0))
            w_recip = sch.op(
                "vector",
                lambda e, par_o=par_o: e.reciprocal(g_rl[:], L_PS[par_o]),
                waits=[lav_val[gpair - 1], bcast_val.get(gslot - 1)], inc="dve")
            pending = (gslot, par_o, w_recip)
            gslot += 1
        kvbuf_free[par] = lav_val[gpair - 1]
    pgs, ppar_o, w_recip = pending
    w_bc = sch.op("tensor",
                  lambda e, ppar_o=ppar_o: e.matmul(BC_PS[ppar_o], g_onesrf[:], g_rl[:],
                                                    start=True, stop=True),
                  waits=[w_recip], inc="pe")
    w_evb = sch.op("vector", lambda e, ppar_o=ppar_o: e.tensor_copy(g_rlb[:], BC_PS[ppar_o]),
                   waits=[w_bc], inc="dve")
    phk, pslot = divmod(pgs, 8)
    w_mu = sch.op(
        "vector",
        lambda e, ppar_o=ppar_o, phk=phk, pslot=pslot: e.tensor_tensor(
            c_oT[:, pslot, phk * 4:(phk + 1) * 4, :],
            O_PS[ppar_o], g_rlb[:], ALU.mult),
        waits=[w_evb], inc="dve")
    norm_done[pgs] = w_mu
    pending = None
    w_attn_done = w_mu
    w_attn_pe = lav_val[gpair - 1]

    wo_r = wo_g.ap().rearrange("(h p) n -> p h n", p=128)
    wo_free = {0: None, 1: None}
    if (gpair - 1) % 2 == 0:
        _psa_last, _psb_last = exp_val[gpair - 1], exp_val[gpair - 2]
    else:
        _psa_last, _psb_last = exp_val[gpair - 2], exp_val[gpair - 1]
    sl_free = [_psa_last, _psa_last, _psb_last, _psb_last]
    xin_free = {0: None, 1: None}
    xrs_free = {0: None, 1: None}
    dxr_val = {}
    sloti = 0
    for ns in range(8):
        par = ns % 2
        osem = "dwo" + str(par)
        sch.dma("sync", a_wo[par][:, 0:16, :],
                wo_r[:, 0:16, ns * 512:(ns + 1) * 512], osem,
                waits=[w_attn_pe if ns < 2 else wo_free[par],
                       ("ccwo", 1) if ns < 2 else None])
        w_wo = sch.dma("sync", a_wo[par][:, 16:32, :],
                       wo_r[:, 16:32, ns * 512:(ns + 1) * 512], osem)
        for tt in range(NT):
            sl = sloti % 4
            par2 = sloti % 2
            sloti += 1
            acc = ACCS[sl]
            w_xi = sch.dma("sync", g_xin[par2][:],
                           x_in[tt * 128:(tt + 1) * 128, ns * 512:(ns + 1) * 512],
                           "dxin" + str(par2),
                           waits=[xin_free[par2],
                                  w_attn_pe if sloti <= 2 else None,
                                  w_attn_done if sloti <= 2 else None])
            last = None
            for h in range(HQ):
                last = sch.op(
                    "tensor",
                    lambda e, acc=acc, par=par, h=h, tt=tt: e.matmul(
                        acc, c_oT[:, tt, h, :], a_wo[par][:, h, :],
                        start=(h == 0), stop=(h == HQ - 1)),
                    waits=[w_wo, sl_free[sl], w_attn_done if sloti == 1 else None]
                    if h == 0 else [],
                    inc="pe" if h == HQ - 1 else None)
            w_add = sch.op(
                "vector",
                lambda e, acc=acc, par2=par2: e.tensor_tensor(
                    g_xrs[par2][:], acc, g_xin[par2][:], ALU.add),
                waits=[last, w_xi, xrs_free[par2]], inc="dve")
            sl_free[sl] = w_add
            xin_free[par2] = w_add
            w_wr = sch.dma("sync",
                           xres_d.ap()[tt * 128:(tt + 1) * 128, ns * 512:(ns + 1) * 512],
                           g_xrs[par2][:], "dxr", waits=[w_add])
            xrs_free[par2] = w_wr
            dxr_val[(ns, tt)] = w_wr
        wo_free[par] = last
    w_outproj_pe = last

    w12_r = w12t_g.ap().rearrange("(r kc p) m -> r p kc m", r=8, p=128)
    w3_r = w3t_g.ap().rearrange("(r mc p) n -> r p mc n", r=8, p=128)
    w12_val = {}
    for q in range(4):
        w12_val[(0, q)] = sch.dma(
            "sync", b_w12[:, q * 8:(q + 1) * 8, :],
            w12_r[0][:, q * 8:(q + 1) * 8, :], "dw12g" + str(q),
            waits=[("ccw12", 1), w_attn_pe])
    w_b3 = sch.dma("gpsimd", t_b3[:], b3.ap().rearrange("(o n) -> o n", o=1), "db3",
                   waits=[w_attn_done])
    w_h2_pe = None
    evac_free = {0: None, 1: None}
    w_h2T = None
    for tt in range(NT):
        w_xr = sch.dma("sync", c_xres[:], xres_d.ap()[tt * 128:(tt + 1) * 128, :], "dxb",
                       waits=[("dxr", sch.counts["dxr"]), w_outproj_pe, w_h2_pe])
        w_bs = None
        for i in range(8):
            w_bs = sch.op(
                "vector",
                lambda e, i=i: e.bn_stats(g_stats[:, i * 6:(i + 1) * 6],
                                          c_xres[:, i * 512:(i + 1) * 512]),
                waits=[w_xr] if i == 0 else [], inc="dve" if i == 7 else None)
        w_ag = sch.op("vector", lambda e: e.bn_aggr(g_mv[:], g_stats[:]),
                      waits=[w_bs], inc="dve")
        w_ln = sch.op("scalar",
                      lambda e: e.activation(g_t1[:], g_mv[:, 1:2], AF.Ln, bias=g_eps[:]),
                      waits=[w_ag], inc="act")
        w_rs = sch.op("scalar",
                      lambda e: e.activation(g_rstd[:], g_t1[:], AF.Exp, scale=-0.5),
                      waits=[w_ln], inc="act")
        w_ap = sch.op(
            "vector",
            lambda e: e.tensor_scalar(c_xres[:], c_xres[:], g_mv[:, 0:1], g_rstd[:],
                                      ALU.subtract, ALU.mult),
            waits=[w_rs], inc="dve")
        for kb in range(4):
            par = kb % 2
            trb = TRB[par]
            last_tr = None
            for kc8 in range(8):
                kc = kb * 8 + kc8
                last_tr = sch.op(
                    "tensor",
                    lambda e, trb=trb, kc8=kc8, kc=kc: e.transpose(
                        trb[:, kc8 * 128:(kc8 + 1) * 128],
                        c_xres[:, kc * 128:(kc + 1) * 128], g_id[:]),
                    waits=[w_ap, evac_free[par]]
                    + ([w_attn_done, sl_free[0], sl_free[1], sl_free[2], sl_free[3]]
                       if tt == 0 and kb < 2 else [])
                    if kc8 == 0 else [],
                    inc="pe" if kc8 == 7 else None)
            w_ev = sch.op(
                "scalar",
                lambda e, trb=trb, kb=kb, tt=tt: e.copy(
                    a_h2T[:, kb * 8:(kb + 1) * 8, tt * 128:(tt + 1) * 128],
                    trb[:].rearrange("p (a q) -> p a q", a=8)),
                waits=[last_tr], inc="act")
            evac_free[par] = w_ev
        w_h2_pe = last_tr
        w_h2T = w_ev

    acc6_free = [None] * 6
    x2s_free = {0: None, 1: None}
    silu_val = {}
    w_w12_pe = {}
    q_done = {}
    sloti = 0
    for r in range(8):
        for q in range(4):
            if (r, q) not in w12_val:
                w12_val[(r, q)] = sch.dma(
                    "sync", b_w12[:, q * 8:(q + 1) * 8, :],
                    w12_r[r][:, q * 8:(q + 1) * 8, :], "dw12g" + str(q),
                    waits=[q_done.get((r - 1, q))])
        for jj in range(8):
            j = r * 8 + jj
            slA = sloti % 6
            slB = (sloti + 1) % 6
            sloti += 2
            accA, accB = ACCS6[slA], ACCS6[slB]
            last = None
            for fc in range(FC):
                qq = fc // 8
                sch.op(
                    "tensor",
                    lambda e, acc=accA, fc=fc, jj=jj: e.matmul(
                        acc, b_w12[:, fc, jj * 128:(jj + 1) * 128],
                        a_h2T[:, fc, 0:512],
                        start=(fc == 0), stop=(fc == FC - 1)),
                    waits=([w12_val[(r, qq)], acc6_free[slA],
                            acc6_free[slB], w_h2T,
                            w_attn_done if sloti <= 6 else None]
                           if fc % 8 == 0 else []))
                last = sch.op(
                    "tensor",
                    lambda e, acc=accB, fc=fc, jj=jj: e.matmul(
                        acc, b_w12[:, fc, jj * 128:(jj + 1) * 128],
                        a_h2T[:, fc, 512:1024],
                        start=(fc == 0), stop=(fc == FC - 1)),
                    inc="pe" if (fc % 8 == 7 and jj == 7) or fc == FC - 1 else None)
                if fc % 8 == 7 and jj == 7:
                    q_done[(r, qq)] = last
            w_w12_pe[r] = last
            for th, acc, sl in ((0, accA, slA), (1, accB, slB)):
                if j < 32:
                    w_si = sch.op(
                        "scalar",
                        lambda e, acc=acc, j=j, th=th: e.activation(
                            c_x4[:, j, th * 512:(th + 1) * 512], acc, AF.Silu,
                            bias=g_b12[:, j:j + 1]),
                        waits=[last, w_b12], inc="act")
                    acc6_free[sl] = w_si
                    silu_val[(j, th)] = w_si
                else:
                    par2 = th
                    w_x2 = sch.op(
                        "scalar",
                        lambda e, acc=acc, j=j, par2=par2: e.activation(
                            g_x2s[par2][:], acc, AF.Identity, bias=g_b12[:, j:j + 1]),
                        waits=[last, x2s_free[par2]], inc="act")
                    acc6_free[sl] = w_x2
                    w_x4 = sch.op(
                        "vector",
                        lambda e, j=j, th=th, par2=par2: e.tensor_tensor(
                            c_x4[:, j - 32, th * 512:(th + 1) * 512],
                            c_x4[:, j - 32, th * 512:(th + 1) * 512],
                            g_x2s[par2][:], ALU.mult),
                        waits=[w_x2, silu_val[(j - 32, th)]], inc="dve")
                    x2s_free[par2] = w_x4
    w_x4_last = w_x4

    w3buf_free = {0: None, 1: None}
    sl_free = [acc6_free[0], acc6_free[1], acc6_free[2], acc6_free[3]]
    xin_free = {0: None, 1: None}
    xrs_free = {0: None, 1: None}
    w_out = None
    sloti = 0
    for ns in range(8):
        par = ns % 2
        w_w3 = sch.dma("sync", b_w3[par][:], w3_r[ns], "dw3g" + str(par),
                       waits=[w_w12_pe[7] if ns < 2 else w3buf_free[par],
                              ("ccw3", 1) if ns < 2 else None])
        for tt in range(NT):
            sl = sloti % 4
            par2 = sloti % 2
            sloti += 1
            acc = ACCS[sl]
            w_xi = sch.dma("sync", g_xin[par2][:],
                           xres_d.ap()[tt * 128:(tt + 1) * 128, ns * 512:(ns + 1) * 512],
                           "dxr3d" + str(par2), waits=[xin_free[par2]])
            last = None
            for mc in range(32):
                last = sch.op(
                    "tensor",
                    lambda e, acc=acc, par=par, mc=mc, tt=tt: e.matmul(
                        acc, c_x4[:, mc, tt * 128:(tt + 1) * 128], b_w3[par][:, mc, :],
                        start=(mc == 0), stop=False),
                    waits=[w_w3, sl_free[sl], w_x4_last if sloti == 1 else None]
                    if mc == 0 else [])
            last = sch.op(
                "tensor",
                lambda e, acc=acc, ns=ns: e.matmul(
                    acc, g_onesr[:], t_b3[0:1, ns * 512:(ns + 1) * 512],
                    start=False, stop=True),
                waits=[w_b3 if sloti == 1 else None], inc="pe")
            w_add = sch.op(
                "vector",
                lambda e, acc=acc, par2=par2: e.tensor_tensor(
                    g_xrs[par2][:], acc, g_xin[par2][:], ALU.add),
                waits=[last, w_xi, xrs_free[par2]], inc="dve")
            sl_free[sl] = w_add
            xin_free[par2] = w_add
            w_out = sch.dma("sync",
                            out[tt * 128:(tt + 1) * 128, ns * 512:(ns + 1) * 512],
                            g_xrs[par2][:], "dout", waits=[w_add])
            xrs_free[par2] = w_out
        w3buf_free[par] = last

    sch.op("sync", lambda e: e.nop(), waits=[w_out])

    with ctx:
        with nc.Block() as block:
            block.tensor(lambda e: sch.replay("tensor", e))
            block.vector(lambda e: sch.replay("vector", e))
            block.scalar(lambda e: sch.replay("scalar", e))
            block.gpsimd(lambda e: sch.replay("gpsimd", e))
            block.sync(lambda e: sch.replay("sync", e))
    return nc


_NC = None


def _get_nc():
    global _NC
    if _NC is None:
        _NC = build_nc()
    return _NC


def _host_tables(c):
    pos = ((np.arange(TLOC) // 128) * NCORE + c) * 128 + (np.arange(TLOC) % 128)
    freqs = 1.0 / (10000.0 ** (np.arange(0, HD, 2, dtype=np.float64) / HD))
    ang = pos[:, None].astype(np.float64) * freqs[None, :]
    cos = np.cos(ang).astype(np.float32)
    sin = np.sin(ang).astype(np.float32)
    Cs = np.empty((128, TLOC), dtype=np.float32)
    Ss = np.empty((128, TLOC), dtype=np.float32)
    Cs[0::2, :] = cos.T
    Cs[1::2, :] = cos.T
    Ss[0::2, :] = -sin.T
    Ss[1::2, :] = sin.T
    tri = np.triu(np.ones((128, 128), dtype=np.float32))
    m = np.empty((8, 128, 512), dtype=ml_dtypes.bfloat16)
    for jj in range(8):
        if jj < c:
            blk = np.ones((128, 128), dtype=np.float32)
        elif jj == c:
            blk = tri
        else:
            blk = np.zeros((128, 128), dtype=np.float32)
        m[jj] = np.tile(blk, (1, 4)).astype(ml_dtypes.bfloat16)
    return Cs, Ss, m


def kernel(L0_attn_x, L0_attn_wq, L0_attn_wk, L0_attn_wv, L0_attn_wo,
           L0_ffn_w12, L0_ffn_b12, L0_ffn_w3, L0_ffn_b3):
    x = np.ascontiguousarray(np.asarray(L0_attn_x, dtype=np.float32).reshape(8192, D))
    wq = np.asarray(L0_attn_wq, dtype=np.float32)
    wk = np.asarray(L0_attn_wk, dtype=np.float32)
    wv = np.asarray(L0_attn_wv, dtype=np.float32)
    wo = np.asarray(L0_attn_wo, dtype=np.float32)
    w12 = np.asarray(L0_ffn_w12, dtype=np.float32)
    b12 = np.ascontiguousarray(np.asarray(L0_ffn_b12, dtype=np.float32))
    w3 = np.asarray(L0_ffn_w3, dtype=np.float32)
    b3 = np.ascontiguousarray(np.asarray(L0_ffn_b3, dtype=np.float32))

    xt = x.reshape(64, 128, D)
    in_maps = []
    for c in range(NCORE):
        Cs, Ss, m = _host_tables(c)
        in_maps.append({
            "x": np.ascontiguousarray(xt[c::NCORE].reshape(TLOC, D)),
            "wqs": np.ascontiguousarray(wq[c * 512:(c + 1) * 512]),
            "wks": np.ascontiguousarray(wk[c * 512:(c + 1) * 512]),
            "wvs": np.ascontiguousarray(wv[c * 512:(c + 1) * 512]),
            "wos": np.ascontiguousarray(wo[c * 512:(c + 1) * 512]),
            "w12s": np.ascontiguousarray(w12[c * 1024:(c + 1) * 1024]),
            "w3s": np.ascontiguousarray(w3[c * 512:(c + 1) * 512]),
            "b12": b12, "b3": b3,
            "ropec": Cs, "ropes": Ss, "masks": m,
        })

    nc = _get_nc()
    res = run_bass_kernel_spmd(nc, in_maps, core_ids=list(range(NCORE)))
    full = np.empty((64, 128, D), dtype=np.float32)
    for c in range(NCORE):
        full[c::NCORE] = res.results[c]["out"].reshape(NT, 128, D)
    return full.reshape(1, 8192, D)


# version 20
# speedup vs baseline: 1.0052x; 1.0052x over previous
import math
import contextlib
import numpy as np
import ml_dtypes
import concourse.bass as bass
import concourse.mybir as mybir
from concourse.bass_utils import run_bass_kernel_spmd

F32 = mybir.dt.float32
BF16 = mybir.dt.bfloat16
AF = mybir.ActivationFunctionType
ALU = mybir.AluOpType

NCORE = 8
NT = 8
TLOC = 1024
D = 4096
FC = 32
HQ = 32
HKV = 8
HD = 128
SM_SCALE = 1.0 / math.sqrt(HD)
LN_EPS = 1e-5



class Sched:

    ENGS = ("tensor", "vector", "scalar", "gpsimd", "sync")

    def __init__(self, nc):
        self.nc = nc
        self.handles = {}
        self.counts = {}
        self.ops = {e: [] for e in self.ENGS}
        self.waited = {e: {} for e in self.ENGS}

    def sem(self, name):
        h = self.nc.alloc_semaphore(name)
        self.handles[name] = h
        self.counts[name] = 0
        return h

    def op(self, eng, fn, waits=(), inc=None, inc_amt=1):
        w = {}
        for item in waits:
            if item is None:
                continue
            s, v = item
            if v is None or v <= 0:
                continue
            if self.waited[eng].get(s, 0) >= v:
                continue
            w[s] = max(w.get(s, 0), v)
        for s, v in w.items():
            self.waited[eng][s] = v
        val = None
        if inc is not None:
            self.counts[inc] += inc_amt
            val = self.counts[inc]
            self.waited[eng][inc] = max(self.waited[eng].get(inc, 0), val)
        self.ops[eng].append((tuple(w.items()), fn, inc, inc_amt))
        return (inc, val) if inc is not None else None

    def dma(self, eng, out, in_, sem, waits=(), **kw):
        return self.op(
            eng,
            lambda e, out=out, in_=in_, kw=kw: e.dma_start(out=out, in_=in_, **kw),
            waits=waits, inc=sem, inc_amt=16)

    def replay(self, eng_name, eng):
        for waits, fn, inc, inc_amt in self.ops[eng_name]:
            for s, v in waits:
                eng.wait_ge(self.handles[s], v)
            ins = fn(eng)
            if inc is not None:
                ins.then_inc(self.handles[inc], inc_amt)


def build_nc(detect_races=True):
    nc = bass.Bass("TRN2", target_bir_lowering=False, dynamic_dma_scratch_size=8192,
                   detect_race_conditions=detect_races)
    OFF_A = (nc.sbuf_base + 63) & ~63
    OFF_B = OFF_A + 65536
    OFF_C = OFF_A + 131072
    OFF_TAB = OFF_A + 196608
    OFF_G = OFF_A + 204800

    x_in = nc.dram_tensor("x", [TLOC, D], F32, kind="ExternalInput")
    wqs = nc.dram_tensor("wqs", [512, D], F32, kind="ExternalInput")
    wks = nc.dram_tensor("wks", [512, 1024], F32, kind="ExternalInput")
    wvs = nc.dram_tensor("wvs", [512, 1024], F32, kind="ExternalInput")
    wos = nc.dram_tensor("wos", [512, D], F32, kind="ExternalInput")
    w12s = nc.dram_tensor("w12s", [1024, D], F32, kind="ExternalInput")
    w3s = nc.dram_tensor("w3s", [512, D], F32, kind="ExternalInput")
    b12 = nc.dram_tensor("b12", [8192], F32, kind="ExternalInput")
    b3 = nc.dram_tensor("b3", [D], F32, kind="ExternalInput")
    ropec = nc.dram_tensor("ropec", [128, TLOC], F32, kind="ExternalInput")
    ropes = nc.dram_tensor("ropes", [128, TLOC], F32, kind="ExternalInput")
    masks = nc.dram_tensor("masks", [8, 128, 512], BF16, kind="ExternalInput")
    out = nc.dram_tensor("out", [TLOC, D], F32, kind="ExternalOutput")

    wq_c = nc.dram_tensor("wq_c", [16384, 128], BF16, kind="Internal")
    wk_c = nc.dram_tensor("wk_c", [4096, 128], BF16, kind="Internal")
    wv_c = nc.dram_tensor("wv_c", [512, 1024], BF16, kind="Internal")
    wo_c = nc.dram_tensor("wo_c", [512, D], BF16, kind="Internal")
    wq_g = nc.dram_tensor("wq_g", [NCORE * 16384, 128], BF16, kind="Internal", addr_space="Shared")
    wk_g = nc.dram_tensor("wk_g", [NCORE * 4096, 128], BF16, kind="Internal", addr_space="Shared")
    wv_g = nc.dram_tensor("wv_g", [D, 1024], BF16, kind="Internal", addr_space="Shared")
    wo_g = nc.dram_tensor("wo_g", [D, D], BF16, kind="Internal", addr_space="Shared")
    w12t_c = nc.dram_tensor("w12t_c", [D, 1024], BF16, kind="Internal")
    w3t_c = nc.dram_tensor("w3t_c", [D, 512], BF16, kind="Internal")
    k_c = nc.dram_tensor("k_c", [1024, 1024], BF16, kind="Internal")
    v_c = nc.dram_tensor("v_c", [1024, 1024], BF16, kind="Internal")
    w12t_g = nc.dram_tensor("w12t_g", [NCORE * D, 1024], BF16, kind="Internal", addr_space="Shared")
    w3t_g = nc.dram_tensor("w3t_g", [NCORE * D, 512], BF16, kind="Internal", addr_space="Shared")
    k_g = nc.dram_tensor("k_g", [8192, 1024], BF16, kind="Internal", addr_space="Shared")
    v_g = nc.dram_tensor("v_g", [8192, 1024], BF16, kind="Internal", addr_space="Shared")
    xres_d = nc.dram_tensor("xres_d", [TLOC, D], F32, kind="Internal")

    id128f = nc.inline_tensor(np.eye(128, dtype=np.float32), "id128f")
    psw = np.zeros((128, 128), dtype=ml_dtypes.bfloat16)
    psw[np.arange(128) ^ 1, np.arange(128)] = 1
    pswap_c = nc.inline_tensor(psw, "pswap_c")
    ones_col_c = nc.inline_tensor(np.ones((128, 1), dtype=ml_dtypes.bfloat16), "ones_col_c")
    ones_row_c = nc.inline_tensor(np.ones((1, 128), dtype=ml_dtypes.bfloat16), "ones_row_c")
    ones_rowf_c = nc.inline_tensor(np.ones((1, 128), dtype=np.float32), "ones_rowf_c")

    SA = nc.alloc_sbuf_tensor_at
    a_xlnT = SA("a_xlnT", [128, FC, 1024], BF16, offset=OFF_A)
    a_kb = [SA(f"a_kb{i}", [128, 64, 128], BF16, offset=OFF_A + i * 32768) for i in range(2)]
    a_vb = [SA(f"a_vb{i}", [128, 64, 128], BF16, offset=OFF_A + i * 32768 + 16384) for i in range(2)]
    a_wo = [SA(f"a_wo{i}", [128, 32, 512], BF16, offset=OFF_A + i * 32768) for i in range(2)]
    a_h2T = SA("a_h2T", [128, FC, 1024], BF16, offset=OFF_A)

    b_wv = SA("b_wv", [128, FC, 512], BF16, offset=OFF_B)
    b_ktl = SA("b_ktl", [128, HKV, 1024], BF16, offset=OFF_B + 32768)
    b_vtl = SA("b_vtl", [128, NT, 1024], BF16, offset=OFF_B + 49152)
    b_qT = SA("b_qT", [128, NT, HQ, 128], BF16, offset=OFF_B)
    b_w12 = SA("b_w12", [128, FC, 1024], BF16, offset=OFF_B)
    b_w3 = [SA(f"b_w3{i}", [128, 32, 512], BF16, offset=OFF_B + i * 32768) for i in range(2)]

    c_stage = SA("c_stage", [128, D], F32, offset=OFF_C)
    c_cast = SA("c_cast", [128, D], BF16, offset=OFF_C + 16384)
    c_band = [SA(f"c_band{i}", [128, 8, 128], BF16, offset=OFF_C + 24576 + i * 2048) for i in range(2)]
    c_x = SA("c_x", [128, D], F32, offset=OFF_C + 28672)
    c_x2 = SA("c_x2", [128, D], F32, offset=OFF_C + 45056)
    c_xres2 = SA("c_xres2", [128, D], F32, offset=OFF_C)
    c_wbuf = [SA(f"c_wbuf{i}", [128, FC, 128], BF16, offset=OFF_C + 45056 + i * 8192) for i in range(2)]
    c_oT = SA("c_oT", [128, NT, HQ, 128], BF16, offset=OFF_C)
    c_xres = SA("c_xres", [128, D], F32, offset=OFF_C + 28672)
    c_x4 = SA("c_x4", [128, 32, 1024], BF16, offset=OFF_C)

    t_ropec = SA("t_ropec", [128, 1024], F32, offset=OFF_TAB)
    t_ropes = SA("t_ropes", [128, 1024], F32, offset=OFF_TAB + 4096)
    t_masks = SA("t_masks", [128, 8, 512], BF16, offset=OFF_TAB)
    t_b3 = SA("t_b3", [1, D], BF16, offset=OFF_TAB)

    gofs = OFF_G
    def galloc(name, shape, dtype):
        nonlocal gofs
        h = SA(name, shape, dtype, offset=gofs)
        nbytes = int(np.prod(shape[1:])) * mybir.dt.size(dtype)
        gofs += (nbytes + 63) & ~63
        return h
    g_id = galloc("g_id", [128, 128], F32)
    g_pswap = galloc("g_pswap", [128, 128], BF16)
    g_onesc = galloc("g_onesc", [128, 1], BF16)
    g_onesr = galloc("g_onesr", [1, 128], BF16)
    g_onesrf = galloc("g_onesrf", [1, 128], F32)
    g_b12 = galloc("g_b12", [128, 64], F32)
    g_b12st = galloc("g_b12st", [64, 128], F32)
    g_eps = galloc("g_eps", [128, 1], F32)
    g_stats = [galloc(f"g_stats{i}", [128, 48], F32) for i in range(2)]
    g_mv = [galloc(f"g_mv{i}", [128, 2], F32) for i in range(2)]
    g_t1 = [galloc(f"g_t1{i}", [128, 1], F32) for i in range(2)]
    g_rstd = [galloc(f"g_rstd{i}", [128, 1], F32) for i in range(2)]
    govl = gofs
    g_kqbf = galloc("g_kqbf", [128, 512], BF16)
    g_sc1 = galloc("g_sc1", [128, 512], F32)
    g_sc2 = galloc("g_sc2", [128, 512], F32)
    gofs = govl
    g_pt = [galloc(f"g_pt{i}", [128, 1024], BF16) for i in range(2)]
    g_ptm = [galloc(f"g_ptm{i}", [128, 1024], BF16) for i in range(2)]
    g_rl = galloc("g_rl", [1, 512], F32)
    g_rlb = galloc("g_rlb", [128, 512], F32)
    gend = gofs
    gofs = govl
    g_x2s = [galloc(f"g_x2s{i}", [128, 512], BF16) for i in range(2)]
    g_xin = [galloc(f"g_xin{i}", [128, 512], F32) for i in range(2)]
    g_xrs = [galloc(f"g_xrs{i}", [128, 512], F32) for i in range(2)]
    gend = max(gend, gofs)
    assert gend <= nc.sbuf_top, f"G overflow: {gend} > {nc.sbuf_top}"

    ctx = contextlib.ExitStack()
    PSA = ctx.enter_context(nc.psum_tensor("PSA", [128, 1024], F32))
    PSB = ctx.enter_context(nc.psum_tensor("PSB", [128, 1024], F32))
    PSC = ctx.enter_context(nc.psum_tensor("PSC", [128, 1024], F32))
    PSD = ctx.enter_context(nc.psum_tensor("PSD", [128, 1024], F32))
    ACCS = [PSA[:, 0:512], PSA[:, 512:1024], PSB[:, 0:512], PSB[:, 512:1024]]
    ACCS6 = ACCS + [PSC[:, 0:512], PSC[:, 512:1024]]
    O_PS = [PSC[:, 0:512], PSD[:, 0:512]]
    L_PS = [PSC[0:1, 512:1024], PSD[0:1, 512:1024]]
    BC_PS = [PSC[:, 512:1024], PSD[:, 512:1024]]
    TRB = [PSD, PSA]

    sch = Sched(nc)
    for s in ("dx0", "dx1", "dwk0", "dwk1", "dwq0", "dwq1", "dwv", "dwo0", "dwo1",
              "dw12g0", "dw12g1", "dw12g2", "dw12g3", "dw3g0", "dw3g1", "dkv0", "dkv1",
              "dxin0", "dxin1", "dxr", "dxb0", "dxb1", "dxr3d0", "dxr3d1", "dout",
              "dtab", "dmsk", "dstg", "dcb", "dck", "dcv",
              "dcwk", "dcwv", "dcwq", "dcwo", "db3",
              "ccwk", "ccwv", "ccwq", "ccwo", "ccw12", "ccw3", "cck", "ccv",
              "pe", "act", "dve"):
        sch.sem(s)

    def all_gather(in_t, out_t, ccname, waits):
        return sch.op(
            "gpsimd",
            lambda e, in_t=in_t, out_t=out_t: e.collective_compute(
                "AllGather", ALU.bypass, replica_groups=[list(range(NCORE))],
                ins=[in_t.ap().opt()], outs=[out_t.ap().opt()]),
            waits=waits, inc=ccname)

    sch.dma("sync", g_id[:], id128f.ap(), "dtab")
    sch.dma("sync", g_pswap[:], pswap_c.ap(), "dtab")
    sch.dma("sync", g_onesc[:], ones_col_c.ap(), "dtab")
    sch.dma("sync", g_onesr[:], ones_row_c.ap(), "dtab")
    sch.dma("sync", g_onesrf[:], ones_rowf_c.ap(), "dtab")
    sch.dma("sync", t_ropec[:], ropec[:, :], "dtab")
    sch.dma("sync", t_ropes[:], ropes[:, :], "dtab")
    w_consts = sch.dma("sync", g_b12st[:], b12.ap().rearrange("(j p) -> j p", p=128), "dtab")
    sch.op("vector", lambda e: e.memset(g_eps[:], LN_EPS), inc="dve")
    w_b12tr = sch.op("tensor",
                     lambda e: e.transpose(PSC[:, 0:64], g_b12st[:], g_id[0:64, 0:64]),
                     waits=[w_consts], inc="pe")
    w_b12 = sch.op("vector", lambda e: e.tensor_copy(g_b12[:], PSC[:, 0:64]),
                   waits=[w_b12tr], inc="dve")

    stage_state = {"free": None, "cast_free": None}

    def cast_gather(Wsrc, cols, Wc, Wg, ccname, blocked_mc=None, defer=False):
        csem = "dc" + ccname[2:]
        w_out = None
        for t in range(4):
            w_st = sch.dma("sync", c_stage[:, 0:cols],
                           Wsrc[t * 128:(t + 1) * 128, :], "dstg",
                           waits=[stage_state["free"]])
            w_cst = sch.op(
                "scalar",
                lambda e, cols=cols: e.copy(c_cast[:, 0:cols], c_stage[:, 0:cols]),
                waits=[w_st, stage_state["cast_free"]], inc="act")
            stage_state["free"] = w_cst
            if blocked_mc is None:
                dst = Wc[t * 128:(t + 1) * 128, :]
                src = c_cast[:, 0:cols]
            else:
                dst = Wc.ap().rearrange("(mc p f) d -> f p mc d",
                                        mc=blocked_mc, p=128, f=4)[t]
                src = c_cast[:, 0:cols].rearrange("p (mc d) -> p mc d", mc=blocked_mc)
            w_out = sch.dma("sync", dst, src, csem, waits=[w_cst])
            stage_state["cast_free"] = w_out
        if defer:
            return w_out
        return all_gather(Wc, Wg, ccname, [w_out])

    cast_gather(wks, 1024, wk_c, wk_g, "ccwk", blocked_mc=8)
    cast_gather(wqs, D, wq_c, wq_g, "ccwq", blocked_mc=32)
    cast_gather(wvs, 1024, wv_c, wv_g, "ccwv")

    lnx_pe = {0: None, 1: None}
    evac_free = {0: None, 1: None}
    w_xlnT = None
    cxb = [c_x, c_x2]
    for tt in range(NT):
        xb = cxb[tt % 2]
        w_x = sch.dma("sync", xb[:], x_in[tt * 128:(tt + 1) * 128, :], "dx" + str(tt % 2),
                      waits=[lnx_pe[tt % 2]])
        pp = tt % 2
        w_bs = None
        for i in range(8):
            w_bs = sch.op(
                "vector",
                lambda e, i=i, xb=xb, pp=pp: e.bn_stats(g_stats[pp][:, i * 6:(i + 1) * 6],
                                                        xb[:, i * 512:(i + 1) * 512]),
                waits=[w_x] if i == 0 else [], inc="dve" if i == 7 else None)
        w_ag = sch.op("vector", lambda e, pp=pp: e.bn_aggr(g_mv[pp][:], g_stats[pp][:]),
                      waits=[w_bs], inc="dve")
        w_ln = sch.op("scalar",
                      lambda e, pp=pp: e.activation(g_t1[pp][:], g_mv[pp][:, 1:2],
                                                    AF.Ln, bias=g_eps[:]),
                      waits=[w_ag], inc="act")
        w_rs = sch.op("scalar",
                      lambda e, pp=pp: e.activation(g_rstd[pp][:], g_t1[pp][:],
                                                    AF.Exp, scale=-0.5),
                      waits=[w_ln], inc="act")
        w_ap = sch.op(
            "vector",
            lambda e, xb=xb, pp=pp: e.tensor_scalar(xb[:], xb[:], g_mv[pp][:, 0:1],
                                                    g_rstd[pp][:],
                                                    ALU.subtract, ALU.mult),
            waits=[w_rs], inc="dve")
        for kb in range(4):
            par = kb % 2
            trb = TRB[par]
            last_tr = None
            for kc8 in range(8):
                kc = kb * 8 + kc8
                last_tr = sch.op(
                    "tensor",
                    lambda e, trb=trb, kc8=kc8, kc=kc, xb=xb: e.transpose(
                        trb[:, kc8 * 128:(kc8 + 1) * 128],
                        xb[:, kc * 128:(kc + 1) * 128], g_id[:]),
                    waits=[w_ap, evac_free[par]] if kc8 == 0 else [],
                    inc="pe" if kc8 == 7 else None)
            w_ev = sch.op(
                "scalar",
                lambda e, trb=trb, kb=kb, tt=tt: e.copy(
                    a_xlnT[:, kb * 8:(kb + 1) * 8, tt * 128:(tt + 1) * 128],
                    trb[:].rearrange("p (a q) -> p a q", a=8)),
                waits=[last_tr], inc="act")
            evac_free[par] = w_ev
        lnx_pe[tt % 2] = last_tr
        w_xlnT = w_ev
    w_lnx_pe = lnx_pe[1]

    acc_free = [None, None, None, None]
    swap_free = w_b12
    kq_free = None
    wbuf_free = {0: None, 1: None}
    state = {"slot": 0, "lastpe": None}

    def proj_rope(n_mc, wsrc_g, wsem, ccname, dst, extra_first):
        nonlocal swap_free, kq_free
        first = True
        for mc in range(n_mc):
            par = mc % 2
            src_r = wsrc_g.ap().rearrange("(r mc p f) d -> mc p r f d",
                                          r=8, mc=n_mc, p=128, f=4)[mc]
            dst_r = c_wbuf[par][:].rearrange("p (r f) d -> p r f d", r=8)
            psem = wsem + str(par)
            sch.dma("sync", dst_r[:, 0:4], src_r[:, 0:4], psem,
                    waits=[wbuf_free[par], (ccname, 1) if mc < 2 else None,
                           w_lnx_pe if mc < 2 else None])
            w_w = sch.dma("sync", dst_r[:, 4:8], src_r[:, 4:8], psem)
            slA = state["slot"] % 4
            slB = (state["slot"] + 1) % 4
            state["slot"] += 2
            accA, accB = ACCS[slA], ACCS[slB]
            fw = [w_w, acc_free[slA], acc_free[slB], w_xlnT]
            if first:
                fw = fw + extra_first
                first = False
            last = None
            for fc in range(FC):
                sch.op(
                    "tensor",
                    lambda e, acc=accA, par=par, fc=fc: e.matmul(
                        acc, c_wbuf[par][:, fc, :], a_xlnT[:, fc, 0:512],
                        start=(fc == 0), stop=(fc == FC - 1)),
                    waits=fw if fc == 0 else [])
                last = sch.op(
                    "tensor",
                    lambda e, acc=accB, par=par, fc=fc: e.matmul(
                        acc, c_wbuf[par][:, fc, :], a_xlnT[:, fc, 512:1024],
                        start=(fc == 0), stop=(fc == FC - 1)),
                    inc="pe" if fc == FC - 1 else None)
            wbuf_free[par] = last
            state["lastpe"] = last
            for th, acc, sl in ((0, accA, slA), (1, accB, slB)):
                w_evv = sch.op("scalar", lambda e, acc=acc: e.copy(g_kqbf[:], acc),
                               waits=[last, kq_free], inc="act")
                w_sw = sch.op(
                    "tensor",
                    lambda e: e.matmul(PSC[:, 0:512], g_pswap[:], g_kqbf[:],
                                       start=True, stop=True),
                    waits=[w_evv, swap_free], inc="pe")
                kq_free = w_sw
                w_t1 = sch.op(
                    "vector",
                    lambda e, th=th: e.tensor_tensor(
                        g_sc1[:], PSC[:, 0:512],
                        t_ropes[:, th * 512:(th + 1) * 512], ALU.mult),
                    waits=[w_sw], inc="dve")
                swap_free = w_t1
                w_t2 = sch.op(
                    "vector",
                    lambda e, acc=acc, th=th: e.tensor_tensor(
                        g_sc2[:], acc, t_ropec[:, th * 512:(th + 1) * 512], ALU.mult),
                    inc="dve")
                acc_free[sl] = w_t2
                w_t3 = sch.op(
                    "vector",
                    lambda e, mc=mc, th=th, dst=dst: e.tensor_tensor(
                        (dst[:, mc, th * 512:(th + 1) * 512] if len(dst.shape) == 3
                         else dst[:, th * 4:(th + 1) * 4, mc, :]),
                        g_sc1[:], g_sc2[:], ALU.add),
                    waits=[w_t1, w_t2], inc="dve")
        return w_t3

    w_klast = proj_rope(HKV, wk_g, "dwk", "ccwk", b_ktl, [])


    tr_free = {0: None, 1: None}
    band_free = {0: None, 1: None}
    w_con = None
    for (src, dst, ntile) in ((w12s, w12t_c, 8), (w3s, w3t_c, 4)):
        for mj in range(ntile):
            w_st = sch.dma("sync", c_stage[:], src[mj * 128:(mj + 1) * 128, :], "dstg",
                           waits=[stage_state["free"]])
            for kb in range(4):
                par = kb % 2
                trb = TRB[par]
                last_tr = None
                for kc8 in range(8):
                    kc = kb * 8 + kc8
                    last_tr = sch.op(
                        "tensor",
                        lambda e, trb=trb, kc8=kc8, kc=kc: e.transpose(
                            trb[:, kc8 * 128:(kc8 + 1) * 128],
                            c_stage[:, kc * 128:(kc + 1) * 128], g_id[:]),
                        waits=[w_st, tr_free[par], evac_free[0], evac_free[1],
                               acc_free[0], acc_free[1]]
                        if kc8 == 0 else [],
                        inc="pe" if kc8 == 7 else None)
                w_ev = sch.op(
                    "scalar",
                    lambda e, trb=trb, par=par: e.copy(
                        c_band[par][:], trb[:].rearrange("p (a q) -> p a q", a=8)),
                    waits=[last_tr, band_free[par]], inc="act")
                tr_free[par] = w_ev
                for half in range(2):
                    w_con = sch.dma(
                        "sync",
                        dst.ap()[kb * 1024 + half * 512:kb * 1024 + (half + 1) * 512,
                                 mj * 128:(mj + 1) * 128].rearrange(
                            "(kc p) m -> p kc m", p=128),
                        c_band[par][:, half * 4:(half + 1) * 4, :], "dcb",
                        waits=[w_ev])
                band_free[par] = w_con
            stage_state["free"] = last_tr

    w_vlast = None
    for dh in range(2):
        src_r = wv_g.ap().rearrange("(fc p) d -> p fc d", p=128)[
            :, :, dh * 512:(dh + 1) * 512]
        sch.dma("sync", b_wv[:, 0:16, :], src_r[:, 0:16, :], "dwv",
                waits=[("ccwv", 1) if dh == 0 else None,
                       state["lastpe"] if dh == 1 else None])
        w_wv = sch.dma("sync", b_wv[:, 16:32, :], src_r[:, 16:32, :], "dwv")
        for tt in range(NT):
            sl = state["slot"] % 4
            state["slot"] += 1
            acc = ACCS[sl]
            last = None
            for fc in range(FC):
                last = sch.op(
                    "tensor",
                    lambda e, acc=acc, fc=fc, tt=tt: e.matmul(
                        acc, a_xlnT[:, fc, tt * 128:(tt + 1) * 128], b_wv[:, fc, 0:512],
                        start=(fc == 0), stop=(fc == FC - 1)),
                    waits=[w_wv, acc_free[sl], tr_free[0], tr_free[1]]
                    if fc == 0 else [],
                    inc="pe" if fc == FC - 1 else None)
            state["lastpe"] = last
            w_vlast = sch.op(
                "scalar",
                lambda e, acc=acc, tt=tt, dh=dh: e.copy(
                    b_vtl[:, tt, dh * 512:(dh + 1) * 512], acc),
                waits=[last], inc="act")
            acc_free[sl] = w_vlast


    w_conk = None
    for hk in range(HKV):
        w_conk = sch.dma("sync", k_c.ap()[hk * 128:(hk + 1) * 128, :],
                         b_ktl[:, hk, :], "dck", waits=[w_klast])
    all_gather(k_c, k_g, "cck", [w_conk])
    w_conv = None
    for tt in range(NT):
        w_conv = sch.dma("sync", v_c.ap()[tt * 128:(tt + 1) * 128, :],
                         b_vtl[:, tt, :], "dcv", waits=[w_vlast])
    all_gather(v_c, v_g, "ccv", [w_conv])

    w_qlast = proj_rope(HQ, wq_g, "dwq", "ccwq", b_qT, [w_conk, w_conv])

    all_gather(w12t_c, w12t_g, "ccw12",
               [w_con, ("dwq0", sch.counts["dwq0"]), ("dwq1", sch.counts["dwq1"])])
    all_gather(w3t_c, w3t_g, "ccw3", [])
    cast_gather(wos, D, wo_c, wo_g, "ccwo")

    w_masks = sch.dma("sync", t_masks[:],
                      masks.ap().rearrange("j k q -> k j q"), "dmsk",
                      waits=[w_qlast])
    kg_r = k_g.ap().rearrange("(r h p) t -> h p r t", r=8, h=8)
    vg_r = v_g.ap().rearrange("(r j p) c -> j p r c", r=8, j=8)

    qk_val = {}
    exp_val = {}
    lav_val = {}
    norm_done = {}
    bcast_val = {}
    kv_val = {}
    kvbuf_free = {0: None, 1: None}
    gpair = 0
    gslot = 0
    pending = None

    def emit_lav(gp, info):
        hk, slot, p, par, par_o, masked, P = info
        src = g_ptm[gp % 2] if masked else g_pt[gp % 2]
        w_src = exp_val[gp] if not masked else mask_val[gp]
        lastw = None
        for cidx in range(2):
            st = (p == 0) and cidx == 0
            sp = (p == P - 1) and cidx == 1
            sch.op(
                "tensor",
                lambda e, par_o=par_o, src=src, cidx=cidx, st=st, sp=sp: e.matmul(
                    L_PS[par_o], g_onesc[:], src[:, cidx * 512:(cidx + 1) * 512],
                    start=st, stop=sp),
                waits=([w_src] + ([norm_done.get(gslot_of[gp] - 2)] if st else []))
                if cidx == 0 else [])
        for cidx in range(2):
            kt = 2 * p + cidx
            st = (p == 0) and cidx == 0
            sp = (p == P - 1) and cidx == 1
            lastw = sch.op(
                "tensor",
                lambda e, par=par, par_o=par_o, src=src, kt=kt, st=st, sp=sp: e.matmul(
                    O_PS[par_o], a_vb[par][:, kt, :], src[:, (kt % 2) * 512:(kt % 2 + 1) * 512],
                    start=st, stop=sp),
                inc="pe" if cidx == 1 else None)
        lav_val[gp] = lastw
        return lastw

    mask_val = {}
    gslot_of = {}
    lav_queue = []

    for hk in range(HKV):
        par = hk % 2
        ksem = "dkv" + str(par)
        for j in range(8):
            w1 = sch.dma("sync", a_kb[par][:, 8 * j:8 * j + 8, :],
                         kg_r[hk][:, :, j * 128:(j + 1) * 128], ksem,
                         waits=[("cck", 1) if hk == 0 and j == 0 else None,
                                ("ccv", 1) if hk == 0 and j == 0 else None,
                                kvbuf_free[par]])
            w2 = sch.dma("sync", a_vb[par][:, 8 * j:8 * j + 8, :],
                         vg_r[j][:, :, hk * 128:(hk + 1) * 128], ksem)
        kv_full = w2
        for slot in range(8):
            P = 4 * slot + 4
            par_o = gslot % 2
            for p in range(P):
                pbuf = gpair % 2
                PS_pair = PSA if pbuf == 0 else PSB
                gslot_of[gpair] = gslot
                masked = p >= P - 4
                sch.op(
                    "tensor",
                    lambda e, PS_pair=PS_pair, par=par, p=p, hk=hk, slot=slot: e.matmul(
                        PS_pair[:, 0:512], a_kb[par][:, 2 * p, :],
                        b_qT[:, slot, hk * 4:(hk + 1) * 4, :],
                        start=True, stop=True),
                    waits=[exp_val.get(gpair - 2), kv_full,
                           w_qlast if gpair == 0 else None])
                w_qk = sch.op(
                    "tensor",
                    lambda e, PS_pair=PS_pair, par=par, p=p, hk=hk, slot=slot: e.matmul(
                        PS_pair[:, 512:1024], a_kb[par][:, 2 * p + 1, :],
                        b_qT[:, slot, hk * 4:(hk + 1) * 4, :],
                        start=True, stop=True),
                    inc="pe")
                qk_val[gpair] = w_qk
                if p == 0 and pending is not None:
                    pgs, ppar_o, w_recip = pending
                    w_bc = sch.op(
                        "tensor",
                        lambda e, ppar_o=ppar_o: e.matmul(
                            BC_PS[ppar_o], g_onesrf[:], g_rl[:], start=True, stop=True),
                        waits=[w_recip], inc="pe")
                    bcast_val[pgs] = w_bc
                    w_evb = sch.op("vector", lambda e, ppar_o=ppar_o: e.tensor_copy(
                        g_rlb[:], BC_PS[ppar_o]), waits=[w_bc], inc="dve")
                    phk, pslot = divmod(pgs, 8)
                    w_mu = sch.op(
                        "vector",
                        lambda e, ppar_o=ppar_o, phk=phk, pslot=pslot: e.tensor_tensor(
                            c_oT[:, pslot, phk * 4:(phk + 1) * 4, :],
                            O_PS[ppar_o], g_rlb[:], ALU.mult),
                        waits=[w_evb] + ([stage_state["free"], stage_state["cast_free"]]
                                         if pgs == 0 else []),
                        inc="dve")
                    norm_done[pgs] = w_mu
                    pending = None
                if len(lav_queue) >= 2:
                    emit_lav(*lav_queue.pop(0))
                w_exp = sch.op(
                    "scalar",
                    lambda e, PS_pair=PS_pair, pbuf=pbuf: e.activation(
                        g_pt[pbuf][:], PS_pair[:], AF.Exp, scale=SM_SCALE),
                    waits=[w_qk, lav_val.get(gpair - 2), w_masks if gpair == 0 else None],
                    inc="act")
                exp_val[gpair] = w_exp
                if masked:
                    jj0 = 2 * p - 8 * slot
                    sch.op(
                        "vector",
                        lambda e, pbuf=pbuf, jj0=jj0: e.tensor_tensor(
                            g_ptm[pbuf][:, 0:512], g_pt[pbuf][:, 0:512],
                            t_masks[:, jj0, :], ALU.mult),
                        waits=[w_exp, lav_val.get(gpair - 2)])
                    w_m = sch.op(
                        "vector",
                        lambda e, pbuf=pbuf, jj0=jj0: e.tensor_tensor(
                            g_ptm[pbuf][:, 512:1024], g_pt[pbuf][:, 512:1024],
                            t_masks[:, jj0 + 1, :], ALU.mult),
                        inc="dve")
                    mask_val[gpair] = w_m
                lav_queue.append((gpair, (hk, slot, p, par, par_o, masked, P)))
                gpair += 1
            while lav_queue:
                emit_lav(*lav_queue.pop(0))
            w_recip = sch.op(
                "vector",
                lambda e, par_o=par_o: e.reciprocal(g_rl[:], L_PS[par_o]),
                waits=[lav_val[gpair - 1], bcast_val.get(gslot - 1)], inc="dve")
            pending = (gslot, par_o, w_recip)
            gslot += 1
        kvbuf_free[par] = lav_val[gpair - 1]
    pgs, ppar_o, w_recip = pending
    w_bc = sch.op("tensor",
                  lambda e, ppar_o=ppar_o: e.matmul(BC_PS[ppar_o], g_onesrf[:], g_rl[:],
                                                    start=True, stop=True),
                  waits=[w_recip], inc="pe")
    w_evb = sch.op("vector", lambda e, ppar_o=ppar_o: e.tensor_copy(g_rlb[:], BC_PS[ppar_o]),
                   waits=[w_bc], inc="dve")
    phk, pslot = divmod(pgs, 8)
    w_mu = sch.op(
        "vector",
        lambda e, ppar_o=ppar_o, phk=phk, pslot=pslot: e.tensor_tensor(
            c_oT[:, pslot, phk * 4:(phk + 1) * 4, :],
            O_PS[ppar_o], g_rlb[:], ALU.mult),
        waits=[w_evb], inc="dve")
    norm_done[pgs] = w_mu
    pending = None
    w_attn_done = w_mu
    w_attn_pe = lav_val[gpair - 1]

    wo_r = wo_g.ap().rearrange("(h p) n -> p h n", p=128)
    wo_free = {0: None, 1: None}
    if (gpair - 1) % 2 == 0:
        _psa_last, _psb_last = exp_val[gpair - 1], exp_val[gpair - 2]
    else:
        _psa_last, _psb_last = exp_val[gpair - 2], exp_val[gpair - 1]
    sl_free = [_psa_last, _psa_last, _psb_last, _psb_last]
    xin_free = {0: None, 1: None}
    xrs_free = {0: None, 1: None}
    dxr_val = {}
    sloti = 0
    for ns in range(8):
        par = ns % 2
        osem = "dwo" + str(par)
        sch.dma("sync", a_wo[par][:, 0:16, :],
                wo_r[:, 0:16, ns * 512:(ns + 1) * 512], osem,
                waits=[w_attn_pe if ns < 2 else wo_free[par],
                       ("ccwo", 1) if ns < 2 else None])
        w_wo = sch.dma("sync", a_wo[par][:, 16:32, :],
                       wo_r[:, 16:32, ns * 512:(ns + 1) * 512], osem)
        for tt in range(NT):
            sl = sloti % 4
            par2 = sloti % 2
            sloti += 1
            acc = ACCS[sl]
            w_xi = sch.dma("sync", g_xin[par2][:],
                           x_in[tt * 128:(tt + 1) * 128, ns * 512:(ns + 1) * 512],
                           "dxin" + str(par2),
                           waits=[xin_free[par2],
                                  w_attn_pe if sloti <= 2 else None,
                                  w_attn_done if sloti <= 2 else None])
            last = None
            for h in range(HQ):
                last = sch.op(
                    "tensor",
                    lambda e, acc=acc, par=par, h=h, tt=tt: e.matmul(
                        acc, c_oT[:, tt, h, :], a_wo[par][:, h, :],
                        start=(h == 0), stop=(h == HQ - 1)),
                    waits=[w_wo, sl_free[sl], w_attn_done if sloti == 1 else None]
                    if h == 0 else [],
                    inc="pe" if h == HQ - 1 else None)
            w_add = sch.op(
                "vector",
                lambda e, acc=acc, par2=par2: e.tensor_tensor(
                    g_xrs[par2][:], acc, g_xin[par2][:], ALU.add),
                waits=[last, w_xi, xrs_free[par2]], inc="dve")
            sl_free[sl] = w_add
            xin_free[par2] = w_add
            w_wr = sch.dma("sync",
                           xres_d.ap()[tt * 128:(tt + 1) * 128, ns * 512:(ns + 1) * 512],
                           g_xrs[par2][:], "dxr", waits=[w_add])
            xrs_free[par2] = w_wr
            dxr_val[(ns, tt)] = w_wr
        wo_free[par] = last
    w_outproj_pe = last

    w12_r = w12t_g.ap().rearrange("(r kc p) m -> r p kc m", r=8, p=128)
    w3_r = w3t_g.ap().rearrange("(r mc p) n -> r p mc n", r=8, p=128)
    w12_val = {}
    for q in range(4):
        w12_val[(0, q)] = sch.dma(
            "sync", b_w12[:, q * 8:(q + 1) * 8, :],
            w12_r[0][:, q * 8:(q + 1) * 8, :], "dw12g" + str(q),
            waits=[("ccw12", 1), w_attn_pe])
    w_b3 = sch.dma("gpsimd", t_b3[:], b3.ap().rearrange("(o n) -> o n", o=1), "db3",
                   waits=[w_attn_done])
    h2_pe = {0: None, 1: None}
    evac_free = {0: None, 1: None}
    w_h2T = None
    cxrb = [c_xres, c_xres2]
    for tt in range(NT):
        xb = cxrb[tt % 2]
        pp = tt % 2
        w_xr = sch.dma("sync", xb[:], xres_d.ap()[tt * 128:(tt + 1) * 128, :],
                       "dxb" + str(pp),
                       waits=[("dxr", sch.counts["dxr"]), w_outproj_pe, h2_pe[pp]])
        w_bs = None
        for i in range(8):
            w_bs = sch.op(
                "vector",
                lambda e, i=i, xb=xb, pp=pp: e.bn_stats(g_stats[pp][:, i * 6:(i + 1) * 6],
                                                        xb[:, i * 512:(i + 1) * 512]),
                waits=[w_xr] if i == 0 else [], inc="dve" if i == 7 else None)
        w_ag = sch.op("vector", lambda e, pp=pp: e.bn_aggr(g_mv[pp][:], g_stats[pp][:]),
                      waits=[w_bs], inc="dve")
        w_ln = sch.op("scalar",
                      lambda e, pp=pp: e.activation(g_t1[pp][:], g_mv[pp][:, 1:2],
                                                    AF.Ln, bias=g_eps[:]),
                      waits=[w_ag], inc="act")
        w_rs = sch.op("scalar",
                      lambda e, pp=pp: e.activation(g_rstd[pp][:], g_t1[pp][:],
                                                    AF.Exp, scale=-0.5),
                      waits=[w_ln], inc="act")
        w_ap = sch.op(
            "vector",
            lambda e, xb=xb, pp=pp: e.tensor_scalar(xb[:], xb[:], g_mv[pp][:, 0:1],
                                                    g_rstd[pp][:],
                                                    ALU.subtract, ALU.mult),
            waits=[w_rs], inc="dve")
        for kb in range(4):
            par = kb % 2
            trb = TRB[par]
            last_tr = None
            for kc8 in range(8):
                kc = kb * 8 + kc8
                last_tr = sch.op(
                    "tensor",
                    lambda e, trb=trb, kc8=kc8, kc=kc, xb=xb: e.transpose(
                        trb[:, kc8 * 128:(kc8 + 1) * 128],
                        xb[:, kc * 128:(kc + 1) * 128], g_id[:]),
                    waits=[w_ap, evac_free[par]]
                    + ([w_attn_done, sl_free[0], sl_free[1], sl_free[2], sl_free[3]]
                       if tt == 0 and kb < 2 else [])
                    if kc8 == 0 else [],
                    inc="pe" if kc8 == 7 else None)
            w_ev = sch.op(
                "scalar",
                lambda e, trb=trb, kb=kb, tt=tt: e.copy(
                    a_h2T[:, kb * 8:(kb + 1) * 8, tt * 128:(tt + 1) * 128],
                    trb[:].rearrange("p (a q) -> p a q", a=8)),
                waits=[last_tr], inc="act")
            evac_free[par] = w_ev
        h2_pe[tt % 2] = last_tr
        w_h2T = w_ev

    acc6_free = [None] * 6
    x2s_free = {0: None, 1: None}
    silu_val = {}
    w_w12_pe = {}
    q_done = {}
    sloti = 0
    for r in range(8):
        for q in range(4):
            if (r, q) not in w12_val:
                w12_val[(r, q)] = sch.dma(
                    "sync", b_w12[:, q * 8:(q + 1) * 8, :],
                    w12_r[r][:, q * 8:(q + 1) * 8, :], "dw12g" + str(q),
                    waits=[q_done.get((r - 1, q))])
        for jj in range(8):
            j = r * 8 + jj
            slA = sloti % 6
            slB = (sloti + 1) % 6
            sloti += 2
            accA, accB = ACCS6[slA], ACCS6[slB]
            last = None
            for fc in range(FC):
                qq = fc // 8
                sch.op(
                    "tensor",
                    lambda e, acc=accA, fc=fc, jj=jj: e.matmul(
                        acc, b_w12[:, fc, jj * 128:(jj + 1) * 128],
                        a_h2T[:, fc, 0:512],
                        start=(fc == 0), stop=(fc == FC - 1)),
                    waits=([w12_val[(r, qq)], acc6_free[slA],
                            acc6_free[slB], w_h2T,
                            w_attn_done if sloti <= 6 else None]
                           if fc % 8 == 0 else []))
                last = sch.op(
                    "tensor",
                    lambda e, acc=accB, fc=fc, jj=jj: e.matmul(
                        acc, b_w12[:, fc, jj * 128:(jj + 1) * 128],
                        a_h2T[:, fc, 512:1024],
                        start=(fc == 0), stop=(fc == FC - 1)),
                    inc="pe" if (fc % 8 == 7 and jj == 7) or fc == FC - 1 else None)
                if fc % 8 == 7 and jj == 7:
                    q_done[(r, qq)] = last
            w_w12_pe[r] = last
            for th, acc, sl in ((0, accA, slA), (1, accB, slB)):
                if j < 32:
                    w_si = sch.op(
                        "scalar",
                        lambda e, acc=acc, j=j, th=th: e.activation(
                            c_x4[:, j, th * 512:(th + 1) * 512], acc, AF.Silu,
                            bias=g_b12[:, j:j + 1]),
                        waits=[last, w_b12], inc="act")
                    acc6_free[sl] = w_si
                    silu_val[(j, th)] = w_si
                else:
                    par2 = th
                    w_x2 = sch.op(
                        "scalar",
                        lambda e, acc=acc, j=j, par2=par2: e.activation(
                            g_x2s[par2][:], acc, AF.Identity, bias=g_b12[:, j:j + 1]),
                        waits=[last, x2s_free[par2]], inc="act")
                    acc6_free[sl] = w_x2
                    w_x4 = sch.op(
                        "vector",
                        lambda e, j=j, th=th, par2=par2: e.tensor_tensor(
                            c_x4[:, j - 32, th * 512:(th + 1) * 512],
                            c_x4[:, j - 32, th * 512:(th + 1) * 512],
                            g_x2s[par2][:], ALU.mult),
                        waits=[w_x2, silu_val[(j - 32, th)]], inc="dve")
                    x2s_free[par2] = w_x4
    w_x4_last = w_x4

    w3buf_free = {0: None, 1: None}
    sl_free = [acc6_free[0], acc6_free[1], acc6_free[2], acc6_free[3]]
    xin_free = {0: None, 1: None}
    xrs_free = {0: None, 1: None}
    w_out = None
    sloti = 0
    for ns in range(8):
        par = ns % 2
        w_w3 = sch.dma("sync", b_w3[par][:], w3_r[ns], "dw3g" + str(par),
                       waits=[w_w12_pe[7] if ns < 2 else w3buf_free[par],
                              ("ccw3", 1) if ns < 2 else None])
        for tt in range(NT):
            sl = sloti % 4
            par2 = sloti % 2
            sloti += 1
            acc = ACCS[sl]
            w_xi = sch.dma("sync", g_xin[par2][:],
                           xres_d.ap()[tt * 128:(tt + 1) * 128, ns * 512:(ns + 1) * 512],
                           "dxr3d" + str(par2), waits=[xin_free[par2]])
            last = None
            for mc in range(32):
                last = sch.op(
                    "tensor",
                    lambda e, acc=acc, par=par, mc=mc, tt=tt: e.matmul(
                        acc, c_x4[:, mc, tt * 128:(tt + 1) * 128], b_w3[par][:, mc, :],
                        start=(mc == 0), stop=False),
                    waits=[w_w3, sl_free[sl], w_x4_last if sloti == 1 else None]
                    if mc == 0 else [])
            last = sch.op(
                "tensor",
                lambda e, acc=acc, ns=ns: e.matmul(
                    acc, g_onesr[:], t_b3[0:1, ns * 512:(ns + 1) * 512],
                    start=False, stop=True),
                waits=[w_b3 if sloti == 1 else None], inc="pe")
            w_add = sch.op(
                "vector",
                lambda e, acc=acc, par2=par2: e.tensor_tensor(
                    g_xrs[par2][:], acc, g_xin[par2][:], ALU.add),
                waits=[last, w_xi, xrs_free[par2]], inc="dve")
            sl_free[sl] = w_add
            xin_free[par2] = w_add
            w_out = sch.dma("sync",
                            out[tt * 128:(tt + 1) * 128, ns * 512:(ns + 1) * 512],
                            g_xrs[par2][:], "dout", waits=[w_add])
            xrs_free[par2] = w_out
        w3buf_free[par] = last

    sch.op("sync", lambda e: e.nop(), waits=[w_out])

    with ctx:
        with nc.Block() as block:
            block.tensor(lambda e: sch.replay("tensor", e))
            block.vector(lambda e: sch.replay("vector", e))
            block.scalar(lambda e: sch.replay("scalar", e))
            block.gpsimd(lambda e: sch.replay("gpsimd", e))
            block.sync(lambda e: sch.replay("sync", e))
    return nc


_NC = None


def _get_nc():
    global _NC
    if _NC is None:
        _NC = build_nc()
    return _NC


def _host_tables(c):
    pos = ((np.arange(TLOC) // 128) * NCORE + c) * 128 + (np.arange(TLOC) % 128)
    freqs = 1.0 / (10000.0 ** (np.arange(0, HD, 2, dtype=np.float64) / HD))
    ang = pos[:, None].astype(np.float64) * freqs[None, :]
    cos = np.cos(ang).astype(np.float32)
    sin = np.sin(ang).astype(np.float32)
    Cs = np.empty((128, TLOC), dtype=np.float32)
    Ss = np.empty((128, TLOC), dtype=np.float32)
    Cs[0::2, :] = cos.T
    Cs[1::2, :] = cos.T
    Ss[0::2, :] = -sin.T
    Ss[1::2, :] = sin.T
    tri = np.triu(np.ones((128, 128), dtype=np.float32))
    m = np.empty((8, 128, 512), dtype=ml_dtypes.bfloat16)
    for jj in range(8):
        if jj < c:
            blk = np.ones((128, 128), dtype=np.float32)
        elif jj == c:
            blk = tri
        else:
            blk = np.zeros((128, 128), dtype=np.float32)
        m[jj] = np.tile(blk, (1, 4)).astype(ml_dtypes.bfloat16)
    return Cs, Ss, m


def kernel(L0_attn_x, L0_attn_wq, L0_attn_wk, L0_attn_wv, L0_attn_wo,
           L0_ffn_w12, L0_ffn_b12, L0_ffn_w3, L0_ffn_b3):
    x = np.ascontiguousarray(np.asarray(L0_attn_x, dtype=np.float32).reshape(8192, D))
    wq = np.asarray(L0_attn_wq, dtype=np.float32)
    wk = np.asarray(L0_attn_wk, dtype=np.float32)
    wv = np.asarray(L0_attn_wv, dtype=np.float32)
    wo = np.asarray(L0_attn_wo, dtype=np.float32)
    w12 = np.asarray(L0_ffn_w12, dtype=np.float32)
    b12 = np.ascontiguousarray(np.asarray(L0_ffn_b12, dtype=np.float32))
    w3 = np.asarray(L0_ffn_w3, dtype=np.float32)
    b3 = np.ascontiguousarray(np.asarray(L0_ffn_b3, dtype=np.float32))

    xt = x.reshape(64, 128, D)
    in_maps = []
    for c in range(NCORE):
        Cs, Ss, m = _host_tables(c)
        in_maps.append({
            "x": np.ascontiguousarray(xt[c::NCORE].reshape(TLOC, D)),
            "wqs": np.ascontiguousarray(wq[c * 512:(c + 1) * 512]),
            "wks": np.ascontiguousarray(wk[c * 512:(c + 1) * 512]),
            "wvs": np.ascontiguousarray(wv[c * 512:(c + 1) * 512]),
            "wos": np.ascontiguousarray(wo[c * 512:(c + 1) * 512]),
            "w12s": np.ascontiguousarray(w12[c * 1024:(c + 1) * 1024]),
            "w3s": np.ascontiguousarray(w3[c * 512:(c + 1) * 512]),
            "b12": b12, "b3": b3,
            "ropec": Cs, "ropes": Ss, "masks": m,
        })

    nc = _get_nc()
    res = run_bass_kernel_spmd(nc, in_maps, core_ids=list(range(NCORE)))
    full = np.empty((64, 128, D), dtype=np.float32)
    for c in range(NCORE):
        full[c::NCORE] = res.results[c]["out"].reshape(NT, 128, D)
    return full.reshape(1, 8192, D)


# version 21
# speedup vs baseline: 1.0470x; 1.0416x over previous
import math
import contextlib
import numpy as np
import ml_dtypes
import concourse.bass as bass
import concourse.mybir as mybir
from concourse.bass_utils import run_bass_kernel_spmd

F32 = mybir.dt.float32
BF16 = mybir.dt.bfloat16
AF = mybir.ActivationFunctionType
ALU = mybir.AluOpType

NCORE = 8
NT = 8
TLOC = 1024
D = 4096
FC = 32
HQ = 32
HKV = 8
HD = 128
SM_SCALE = 1.0 / math.sqrt(HD)
LN_EPS = 1e-5



class Sched:

    ENGS = ("tensor", "vector", "scalar", "gpsimd", "sync")

    def __init__(self, nc):
        self.nc = nc
        self.handles = {}
        self.counts = {}
        self.ops = {e: [] for e in self.ENGS}
        self.waited = {e: {} for e in self.ENGS}

    def sem(self, name):
        h = self.nc.alloc_semaphore(name)
        self.handles[name] = h
        self.counts[name] = 0
        return h

    def op(self, eng, fn, waits=(), inc=None, inc_amt=1):
        w = {}
        for item in waits:
            if item is None:
                continue
            s, v = item
            if v is None or v <= 0:
                continue
            if self.waited[eng].get(s, 0) >= v:
                continue
            w[s] = max(w.get(s, 0), v)
        for s, v in w.items():
            self.waited[eng][s] = v
        val = None
        if inc is not None:
            self.counts[inc] += inc_amt
            val = self.counts[inc]
            self.waited[eng][inc] = max(self.waited[eng].get(inc, 0), val)
        self.ops[eng].append((tuple(w.items()), fn, inc, inc_amt))
        return (inc, val) if inc is not None else None

    def dma(self, eng, out, in_, sem, waits=(), **kw):
        return self.op(
            eng,
            lambda e, out=out, in_=in_, kw=kw: e.dma_start(out=out, in_=in_, **kw),
            waits=waits, inc=sem, inc_amt=16)

    def replay(self, eng_name, eng):
        for waits, fn, inc, inc_amt in self.ops[eng_name]:
            for s, v in waits:
                eng.wait_ge(self.handles[s], v)
            ins = fn(eng)
            if inc is not None:
                ins.then_inc(self.handles[inc], inc_amt)


def build_nc(detect_races=True):
    nc = bass.Bass("TRN2", target_bir_lowering=False, dynamic_dma_scratch_size=8192,
                   detect_race_conditions=detect_races)
    OFF_A = (nc.sbuf_base + 63) & ~63
    OFF_B = OFF_A + 65536
    OFF_C = OFF_A + 131072
    OFF_TAB = OFF_A + 196608
    OFF_G = OFF_A + 204800

    x_in = nc.dram_tensor("x", [TLOC, D], F32, kind="ExternalInput")
    wqs = nc.dram_tensor("wqs", [512, D], F32, kind="ExternalInput")
    wks = nc.dram_tensor("wks", [512, 1024], F32, kind="ExternalInput")
    wvs = nc.dram_tensor("wvs", [512, 1024], F32, kind="ExternalInput")
    wos = nc.dram_tensor("wos", [512, D], F32, kind="ExternalInput")
    w12s = nc.dram_tensor("w12s", [1024, D], F32, kind="ExternalInput")
    w3s = nc.dram_tensor("w3s", [512, D], F32, kind="ExternalInput")
    b12 = nc.dram_tensor("b12", [8192], F32, kind="ExternalInput")
    b3 = nc.dram_tensor("b3", [D], F32, kind="ExternalInput")
    ropec = nc.dram_tensor("ropec", [128, TLOC], F32, kind="ExternalInput")
    ropes = nc.dram_tensor("ropes", [128, TLOC], F32, kind="ExternalInput")
    masks = nc.dram_tensor("masks", [8, 128, 512], BF16, kind="ExternalInput")
    out = nc.dram_tensor("out", [TLOC, D], F32, kind="ExternalOutput")

    wq_c = nc.dram_tensor("wq_c", [16384, 128], BF16, kind="Internal")
    wk_c = nc.dram_tensor("wk_c", [4096, 128], BF16, kind="Internal")
    wv_c = nc.dram_tensor("wv_c", [512, 1024], BF16, kind="Internal")
    wo_c = nc.dram_tensor("wo_c", [512, D], BF16, kind="Internal")
    wq_g = nc.dram_tensor("wq_g", [NCORE * 16384, 128], BF16, kind="Internal", addr_space="Shared")
    wk_g = nc.dram_tensor("wk_g", [NCORE * 4096, 128], BF16, kind="Internal", addr_space="Shared")
    wv_g = nc.dram_tensor("wv_g", [D, 1024], BF16, kind="Internal", addr_space="Shared")
    wo_g = nc.dram_tensor("wo_g", [D, D], BF16, kind="Internal", addr_space="Shared")
    w12t_c = nc.dram_tensor("w12t_c", [D, 1024], BF16, kind="Internal")
    w3t_c = nc.dram_tensor("w3t_c", [D, 512], BF16, kind="Internal")
    k_c = nc.dram_tensor("k_c", [1024, 1024], BF16, kind="Internal")
    v_c = nc.dram_tensor("v_c", [1024, 1024], BF16, kind="Internal")
    w12t_g = nc.dram_tensor("w12t_g", [NCORE * D, 1024], BF16, kind="Internal", addr_space="Shared")
    w3t_g = nc.dram_tensor("w3t_g", [NCORE * D, 512], BF16, kind="Internal", addr_space="Shared")
    k_g = nc.dram_tensor("k_g", [8192, 1024], BF16, kind="Internal", addr_space="Shared")
    v_g = nc.dram_tensor("v_g", [8192, 1024], BF16, kind="Internal", addr_space="Shared")
    xres_d = nc.dram_tensor("xres_d", [TLOC, D], F32, kind="Internal")

    id128f = nc.inline_tensor(np.eye(128, dtype=np.float32), "id128f")
    psw = np.zeros((128, 128), dtype=ml_dtypes.bfloat16)
    psw[np.arange(128) ^ 1, np.arange(128)] = 1
    pswap_c = nc.inline_tensor(psw, "pswap_c")
    ones_col_c = nc.inline_tensor(np.ones((128, 1), dtype=ml_dtypes.bfloat16), "ones_col_c")
    ones_row_c = nc.inline_tensor(np.ones((1, 128), dtype=ml_dtypes.bfloat16), "ones_row_c")
    ones_rowf_c = nc.inline_tensor(np.ones((1, 128), dtype=np.float32), "ones_rowf_c")

    SA = nc.alloc_sbuf_tensor_at
    a_xlnT = SA("a_xlnT", [128, FC, 1024], BF16, offset=OFF_A)
    a_kb = [SA(f"a_kb{i}", [128, 64, 128], BF16, offset=OFF_A + i * 32768) for i in range(2)]
    a_vb = [SA(f"a_vb{i}", [128, 64, 128], BF16, offset=OFF_A + i * 32768 + 16384) for i in range(2)]
    a_wo = [SA(f"a_wo{i}", [128, 32, 512], BF16, offset=OFF_A + i * 32768) for i in range(2)]
    a_h2T = SA("a_h2T", [128, FC, 1024], BF16, offset=OFF_A)

    b_wv = SA("b_wv", [128, FC, 512], BF16, offset=OFF_B)
    b_ktl = SA("b_ktl", [128, HKV, 1024], BF16, offset=OFF_B + 32768)
    b_vtl = SA("b_vtl", [128, NT, 1024], BF16, offset=OFF_B + 49152)
    b_qT = SA("b_qT", [128, NT, HQ, 128], BF16, offset=OFF_B)
    b_w12 = SA("b_w12", [128, FC, 1024], BF16, offset=OFF_B)
    b_w3 = [SA(f"b_w3{i}", [128, 32, 512], BF16, offset=OFF_B + i * 32768) for i in range(2)]

    c_stage = SA("c_stage", [128, D], F32, offset=OFF_C)
    c_cast = SA("c_cast", [128, D], BF16, offset=OFF_C + 16384)
    c_band = [SA(f"c_band{i}", [128, 8, 128], BF16, offset=OFF_C + 24576 + i * 2048) for i in range(2)]
    c_x = SA("c_x", [128, D], F32, offset=OFF_C + 28672)
    c_x2 = SA("c_x2", [128, D], F32, offset=OFF_C + 45056)
    c_xres2 = SA("c_xres2", [128, D], F32, offset=OFF_C)
    c_wbuf = [SA(f"c_wbuf{i}", [128, FC, 128], BF16, offset=OFF_C + 45056 + i * 8192) for i in range(2)]
    c_oT = SA("c_oT", [128, NT, HQ, 128], BF16, offset=OFF_C)
    c_xres = SA("c_xres", [128, D], F32, offset=OFF_C + 28672)
    c_x4 = SA("c_x4", [128, 32, 1024], BF16, offset=OFF_C)

    t_ropec = SA("t_ropec", [128, 1024], F32, offset=OFF_TAB)
    t_ropes = SA("t_ropes", [128, 1024], F32, offset=OFF_TAB + 4096)
    t_masks = SA("t_masks", [128, 8, 512], BF16, offset=OFF_TAB)
    t_b3 = SA("t_b3", [1, D], BF16, offset=OFF_TAB)

    gofs = OFF_G
    def galloc(name, shape, dtype):
        nonlocal gofs
        h = SA(name, shape, dtype, offset=gofs)
        nbytes = int(np.prod(shape[1:])) * mybir.dt.size(dtype)
        gofs += (nbytes + 63) & ~63
        return h
    g_id = galloc("g_id", [128, 128], F32)
    g_pswap = galloc("g_pswap", [128, 128], BF16)
    g_onesc = galloc("g_onesc", [128, 1], BF16)
    g_onesr = galloc("g_onesr", [1, 128], BF16)
    g_onesrf = galloc("g_onesrf", [1, 128], F32)
    g_b12 = galloc("g_b12", [128, 64], F32)
    g_b12st = galloc("g_b12st", [64, 128], F32)
    g_eps = galloc("g_eps", [128, 1], F32)
    g_stats = [galloc(f"g_stats{i}", [128, 48], F32) for i in range(2)]
    g_mv = [galloc(f"g_mv{i}", [128, 2], F32) for i in range(2)]
    g_t1 = [galloc(f"g_t1{i}", [128, 1], F32) for i in range(2)]
    g_rstd = [galloc(f"g_rstd{i}", [128, 1], F32) for i in range(2)]
    govl = gofs
    g_kqbf = galloc("g_kqbf", [128, 512], BF16)
    g_sc1 = galloc("g_sc1", [128, 512], F32)
    g_sc2 = galloc("g_sc2", [128, 512], F32)
    gofs = govl
    g_pt = [galloc(f"g_pt{i}", [128, 1024], BF16) for i in range(2)]
    g_ptm = [galloc(f"g_ptm{i}", [128, 1024], BF16) for i in range(2)]
    g_rl = galloc("g_rl", [1, 512], F32)
    g_rlb = galloc("g_rlb", [128, 512], F32)
    gend = gofs
    gofs = govl
    g_x2s = [galloc(f"g_x2s{i}", [128, 512], BF16) for i in range(2)]
    g_xin = [galloc(f"g_xin{i}", [128, 512], F32) for i in range(2)]
    g_xrs = [galloc(f"g_xrs{i}", [128, 512], F32) for i in range(2)]
    gend = max(gend, gofs)
    assert gend <= nc.sbuf_top, f"G overflow: {gend} > {nc.sbuf_top}"

    ctx = contextlib.ExitStack()
    PSA = ctx.enter_context(nc.psum_tensor("PSA", [128, 1024], F32))
    PSB = ctx.enter_context(nc.psum_tensor("PSB", [128, 1024], F32))
    PSC = ctx.enter_context(nc.psum_tensor("PSC", [128, 1024], F32))
    PSD = ctx.enter_context(nc.psum_tensor("PSD", [128, 1024], F32))
    ACCS = [PSA[:, 0:512], PSA[:, 512:1024], PSB[:, 0:512], PSB[:, 512:1024]]
    ACCS6 = ACCS + [PSC[:, 0:512], PSC[:, 512:1024]]
    O_PS = [PSC[:, 0:512], PSD[:, 0:512]]
    L_PS = [PSC[0:1, 512:1024], PSD[0:1, 512:1024]]
    BC_PS = [PSC[:, 512:1024], PSD[:, 512:1024]]
    TRB = [PSD, PSA]

    sch = Sched(nc)
    for s in ("dx0", "dx1", "dwk0", "dwk1", "dwq0", "dwq1", "dwv", "dwo0", "dwo1",
              "dw12g0", "dw12g1", "dw12g2", "dw12g3", "dw3g0", "dw3g1", "dkv0", "dkv1",
              "dxin0", "dxin1", "dxr", "dxb0", "dxb1", "dxr3d0", "dxr3d1", "dout",
              "dtab", "dmsk", "dstg", "dcb", "dck", "dcv",
              "dcwk", "dcwv", "dcwq", "dcwo", "db3",
              "ccwk", "ccwv", "ccwq", "ccwo", "ccw12", "ccw3", "cck", "ccv",
              "pe", "act", "dve"):
        sch.sem(s)

    def all_gather(in_t, out_t, ccname, waits):
        return sch.op(
            "gpsimd",
            lambda e, in_t=in_t, out_t=out_t: e.collective_compute(
                "AllGather", ALU.bypass, replica_groups=[list(range(NCORE))],
                ins=[in_t.ap().opt()], outs=[out_t.ap().opt()]),
            waits=waits, inc=ccname)

    def all_gather_ap(in_ap, out_ap, ccname, waits):
        return sch.op(
            "gpsimd",
            lambda e, in_ap=in_ap, out_ap=out_ap: e.collective_compute(
                "AllGather", ALU.bypass, replica_groups=[list(range(NCORE))],
                ins=[in_ap.opt()], outs=[out_ap.opt()]),
            waits=waits, inc=ccname)

    sch.dma("sync", g_id[:], id128f.ap(), "dtab")
    sch.dma("sync", g_pswap[:], pswap_c.ap(), "dtab")
    sch.dma("sync", g_onesc[:], ones_col_c.ap(), "dtab")
    sch.dma("sync", g_onesr[:], ones_row_c.ap(), "dtab")
    sch.dma("sync", g_onesrf[:], ones_rowf_c.ap(), "dtab")
    sch.dma("sync", t_ropec[:], ropec[:, :], "dtab")
    sch.dma("sync", t_ropes[:], ropes[:, :], "dtab")
    w_consts = sch.dma("sync", g_b12st[:], b12.ap().rearrange("(j p) -> j p", p=128), "dtab")
    sch.op("vector", lambda e: e.memset(g_eps[:], LN_EPS), inc="dve")
    w_b12tr = sch.op("tensor",
                     lambda e: e.transpose(PSC[:, 0:64], g_b12st[:], g_id[0:64, 0:64]),
                     waits=[w_consts], inc="pe")
    w_b12 = sch.op("vector", lambda e: e.tensor_copy(g_b12[:], PSC[:, 0:64]),
                   waits=[w_b12tr], inc="dve")

    stage_state = {"free": None, "cast_free": None}

    def cast_gather(Wsrc, cols, Wc, Wg, ccname, blocked_mc=None, defer=False):
        csem = "dc" + ccname[2:]
        w_out = None
        for t in range(4):
            w_st = sch.dma("sync", c_stage[:, 0:cols],
                           Wsrc[t * 128:(t + 1) * 128, :], "dstg",
                           waits=[stage_state["free"]])
            w_cst = sch.op(
                "scalar",
                lambda e, cols=cols: e.copy(c_cast[:, 0:cols], c_stage[:, 0:cols]),
                waits=[w_st, stage_state["cast_free"]], inc="act")
            stage_state["free"] = w_cst
            if blocked_mc is None:
                dst = Wc[t * 128:(t + 1) * 128, :]
                src = c_cast[:, 0:cols]
            else:
                dst = Wc.ap().rearrange("(mc p f) d -> f p mc d",
                                        mc=blocked_mc, p=128, f=4)[t]
                src = c_cast[:, 0:cols].rearrange("p (mc d) -> p mc d", mc=blocked_mc)
            w_out = sch.dma("sync", dst, src, csem, waits=[w_cst])
            stage_state["cast_free"] = w_out
        if defer:
            return w_out
        return all_gather(Wc, Wg, ccname, [w_out])

    cast_gather(wks, 1024, wk_c, wk_g, "ccwk", blocked_mc=8)
    cast_gather(wqs, D, wq_c, wq_g, "ccwq", blocked_mc=32)
    cast_gather(wvs, 1024, wv_c, wv_g, "ccwv")

    lnx_pe = {0: None, 1: None}
    evac_free = {0: None, 1: None}
    w_xlnT = None
    cxb = [c_x, c_x2]
    for tt in range(NT):
        xb = cxb[tt % 2]
        w_x = sch.dma("sync", xb[:], x_in[tt * 128:(tt + 1) * 128, :], "dx" + str(tt % 2),
                      waits=[lnx_pe[tt % 2]])
        pp = tt % 2
        w_bs = None
        for i in range(8):
            w_bs = sch.op(
                "vector",
                lambda e, i=i, xb=xb, pp=pp: e.bn_stats(g_stats[pp][:, i * 6:(i + 1) * 6],
                                                        xb[:, i * 512:(i + 1) * 512]),
                waits=[w_x] if i == 0 else [], inc="dve" if i == 7 else None)
        w_ag = sch.op("vector", lambda e, pp=pp: e.bn_aggr(g_mv[pp][:], g_stats[pp][:]),
                      waits=[w_bs], inc="dve")
        w_ln = sch.op("scalar",
                      lambda e, pp=pp: e.activation(g_t1[pp][:], g_mv[pp][:, 1:2],
                                                    AF.Ln, bias=g_eps[:]),
                      waits=[w_ag], inc="act")
        w_rs = sch.op("scalar",
                      lambda e, pp=pp: e.activation(g_rstd[pp][:], g_t1[pp][:],
                                                    AF.Exp, scale=-0.5),
                      waits=[w_ln], inc="act")
        w_ap = sch.op(
            "vector",
            lambda e, xb=xb, pp=pp: e.tensor_scalar(xb[:], xb[:], g_mv[pp][:, 0:1],
                                                    g_rstd[pp][:],
                                                    ALU.subtract, ALU.mult),
            waits=[w_rs], inc="dve")
        for kb in range(4):
            par = kb % 2
            trb = TRB[par]
            last_tr = None
            for kc8 in range(8):
                kc = kb * 8 + kc8
                last_tr = sch.op(
                    "tensor",
                    lambda e, trb=trb, kc8=kc8, kc=kc, xb=xb: e.transpose(
                        trb[:, kc8 * 128:(kc8 + 1) * 128],
                        xb[:, kc * 128:(kc + 1) * 128], g_id[:]),
                    waits=[w_ap, evac_free[par]] if kc8 == 0 else [],
                    inc="pe" if kc8 == 7 else None)
            w_ev = sch.op(
                "scalar",
                lambda e, trb=trb, kb=kb, tt=tt: e.copy(
                    a_xlnT[:, kb * 8:(kb + 1) * 8, tt * 128:(tt + 1) * 128],
                    trb[:].rearrange("p (a q) -> p a q", a=8)),
                waits=[last_tr], inc="act")
            evac_free[par] = w_ev
        lnx_pe[tt % 2] = last_tr
        w_xlnT = w_ev
    w_lnx_pe = lnx_pe[1]

    acc_free = [None, None, None, None]
    swap_free = w_b12
    kq_free = None
    wbuf_free = {0: None, 1: None}
    state = {"slot": 0, "lastpe": None}

    def proj_rope(n_mc, wsrc_g, wsem, ccname, dst, extra_first):
        nonlocal swap_free, kq_free
        first = True
        for mc in range(n_mc):
            par = mc % 2
            src_r = wsrc_g.ap().rearrange("(r mc p f) d -> mc p r f d",
                                          r=8, mc=n_mc, p=128, f=4)[mc]
            dst_r = c_wbuf[par][:].rearrange("p (r f) d -> p r f d", r=8)
            psem = wsem + str(par)
            sch.dma("sync", dst_r[:, 0:4], src_r[:, 0:4], psem,
                    waits=[wbuf_free[par], (ccname, 1) if mc < 2 else None,
                           w_lnx_pe if mc < 2 else None])
            w_w = sch.dma("sync", dst_r[:, 4:8], src_r[:, 4:8], psem)
            slA = state["slot"] % 4
            slB = (state["slot"] + 1) % 4
            state["slot"] += 2
            accA, accB = ACCS[slA], ACCS[slB]
            fw = [w_w, acc_free[slA], acc_free[slB], w_xlnT]
            if first:
                fw = fw + extra_first
                first = False
            last = None
            for fc in range(FC):
                sch.op(
                    "tensor",
                    lambda e, acc=accA, par=par, fc=fc: e.matmul(
                        acc, c_wbuf[par][:, fc, :], a_xlnT[:, fc, 0:512],
                        start=(fc == 0), stop=(fc == FC - 1)),
                    waits=fw if fc == 0 else [])
                last = sch.op(
                    "tensor",
                    lambda e, acc=accB, par=par, fc=fc: e.matmul(
                        acc, c_wbuf[par][:, fc, :], a_xlnT[:, fc, 512:1024],
                        start=(fc == 0), stop=(fc == FC - 1)),
                    inc="pe" if fc == FC - 1 else None)
            wbuf_free[par] = last
            state["lastpe"] = last
            for th, acc, sl in ((0, accA, slA), (1, accB, slB)):
                w_evv = sch.op("scalar", lambda e, acc=acc: e.copy(g_kqbf[:], acc),
                               waits=[last, kq_free], inc="act")
                w_sw = sch.op(
                    "tensor",
                    lambda e: e.matmul(PSC[:, 0:512], g_pswap[:], g_kqbf[:],
                                       start=True, stop=True),
                    waits=[w_evv, swap_free], inc="pe")
                kq_free = w_sw
                w_t1 = sch.op(
                    "vector",
                    lambda e, th=th: e.tensor_tensor(
                        g_sc1[:], PSC[:, 0:512],
                        t_ropes[:, th * 512:(th + 1) * 512], ALU.mult),
                    waits=[w_sw], inc="dve")
                swap_free = w_t1
                w_t2 = sch.op(
                    "vector",
                    lambda e, acc=acc, th=th: e.tensor_tensor(
                        g_sc2[:], acc, t_ropec[:, th * 512:(th + 1) * 512], ALU.mult),
                    inc="dve")
                acc_free[sl] = w_t2
                w_t3 = sch.op(
                    "vector",
                    lambda e, mc=mc, th=th, dst=dst: e.tensor_tensor(
                        (dst[:, mc, th * 512:(th + 1) * 512] if len(dst.shape) == 3
                         else dst[:, th * 4:(th + 1) * 4, mc, :]),
                        g_sc1[:], g_sc2[:], ALU.add),
                    waits=[w_t1, w_t2], inc="dve")
        return w_t3

    w_klast = proj_rope(HKV, wk_g, "dwk", "ccwk", b_ktl, [])


    tr_free = {0: None, 1: None}
    band_free = {0: None, 1: None}
    w_con = None
    for (src, dst, ntile) in ((w12s, w12t_c, 8), (w3s, w3t_c, 4)):
        for mj in range(ntile):
            w_st = sch.dma("sync", c_stage[:], src[mj * 128:(mj + 1) * 128, :], "dstg",
                           waits=[stage_state["free"]])
            for kb in range(4):
                par = kb % 2
                trb = TRB[par]
                last_tr = None
                for kc8 in range(8):
                    kc = kb * 8 + kc8
                    last_tr = sch.op(
                        "tensor",
                        lambda e, trb=trb, kc8=kc8, kc=kc: e.transpose(
                            trb[:, kc8 * 128:(kc8 + 1) * 128],
                            c_stage[:, kc * 128:(kc + 1) * 128], g_id[:]),
                        waits=[w_st, tr_free[par], evac_free[0], evac_free[1],
                               acc_free[0], acc_free[1]]
                        if kc8 == 0 else [],
                        inc="pe" if kc8 == 7 else None)
                w_ev = sch.op(
                    "scalar",
                    lambda e, trb=trb, par=par: e.copy(
                        c_band[par][:], trb[:].rearrange("p (a q) -> p a q", a=8)),
                    waits=[last_tr, band_free[par]], inc="act")
                tr_free[par] = w_ev
                for half in range(2):
                    w_con = sch.dma(
                        "sync",
                        dst.ap()[kb * 1024 + half * 512:kb * 1024 + (half + 1) * 512,
                                 mj * 128:(mj + 1) * 128].rearrange(
                            "(kc p) m -> p kc m", p=128),
                        c_band[par][:, half * 4:(half + 1) * 4, :], "dcb",
                        waits=[w_ev])
                band_free[par] = w_con
            stage_state["free"] = last_tr

    w_vlast = None
    for dh in range(2):
        src_r = wv_g.ap().rearrange("(fc p) d -> p fc d", p=128)[
            :, :, dh * 512:(dh + 1) * 512]
        sch.dma("sync", b_wv[:, 0:16, :], src_r[:, 0:16, :], "dwv",
                waits=[("ccwv", 1) if dh == 0 else None,
                       state["lastpe"] if dh == 1 else None])
        w_wv = sch.dma("sync", b_wv[:, 16:32, :], src_r[:, 16:32, :], "dwv")
        for tt in range(NT):
            sl = state["slot"] % 4
            state["slot"] += 1
            acc = ACCS[sl]
            last = None
            for fc in range(FC):
                last = sch.op(
                    "tensor",
                    lambda e, acc=acc, fc=fc, tt=tt: e.matmul(
                        acc, a_xlnT[:, fc, tt * 128:(tt + 1) * 128], b_wv[:, fc, 0:512],
                        start=(fc == 0), stop=(fc == FC - 1)),
                    waits=[w_wv, acc_free[sl], tr_free[0], tr_free[1]]
                    if fc == 0 else [],
                    inc="pe" if fc == FC - 1 else None)
            state["lastpe"] = last
            w_vlast = sch.op(
                "scalar",
                lambda e, acc=acc, tt=tt, dh=dh: e.copy(
                    b_vtl[:, tt, dh * 512:(dh + 1) * 512], acc),
                waits=[last], inc="act")
            acc_free[sl] = w_vlast


    conk = []
    for hkk in range(HKV):
        conk.append(sch.dma("sync", k_c.ap()[hkk * 128:(hkk + 1) * 128, :],
                            b_ktl[:, hkk, :], "dck", waits=[w_klast]))
    conv = []
    for tt in range(NT):
        conv.append(sch.dma("sync", v_c.ap()[tt * 128:(tt + 1) * 128, :],
                            b_vtl[:, tt, :], "dcv", waits=[w_vlast]))
    for cch in range(4):
        all_gather_ap(k_c.ap()[cch * 256:(cch + 1) * 256, :],
                      k_g.ap()[cch * 2048:(cch + 1) * 2048, :], "cck",
                      [conk[2 * cch + 1]])
        all_gather_ap(v_c.ap()[cch * 256:(cch + 1) * 256, :],
                      v_g.ap()[cch * 2048:(cch + 1) * 2048, :], "ccv",
                      [conv[2 * cch + 1]])
    w_conk = conk[-1]
    w_conv = conv[-1]

    w_qlast = proj_rope(HQ, wq_g, "dwq", "ccwq", b_qT, [w_conk, w_conv])

    all_gather(w12t_c, w12t_g, "ccw12",
               [w_con, ("dwq0", sch.counts["dwq0"]), ("dwq1", sch.counts["dwq1"])])
    all_gather(w3t_c, w3t_g, "ccw3", [])
    cast_gather(wos, D, wo_c, wo_g, "ccwo")

    w_masks = sch.dma("sync", t_masks[:],
                      masks.ap().rearrange("j k q -> k j q"), "dmsk",
                      waits=[w_qlast])
    kg_r = k_g.ap().rearrange("(c r hh p) t -> c hh p r t", c=4, r=8, hh=2)
    vg_r = v_g.ap().rearrange("(c r jj p) d -> c jj p r d", c=4, r=8, jj=2)

    qk_val = {}
    exp_val = {}
    lav_val = {}
    norm_done = {}
    bcast_val = {}
    kv_val = {}
    kvbuf_free = {0: None, 1: None}
    gpair = 0
    gslot = 0
    pending = None

    def emit_lav(gp, info):
        hk, slot, p, par, par_o, masked, P = info
        src = g_ptm[gp % 2] if masked else g_pt[gp % 2]
        w_src = exp_val[gp] if not masked else mask_val[gp]
        lastw = None
        for cidx in range(2):
            st = (p == 0) and cidx == 0
            sp = (p == P - 1) and cidx == 1
            sch.op(
                "tensor",
                lambda e, par_o=par_o, src=src, cidx=cidx, st=st, sp=sp: e.matmul(
                    L_PS[par_o], g_onesc[:], src[:, cidx * 512:(cidx + 1) * 512],
                    start=st, stop=sp),
                waits=([w_src] + ([norm_done.get(gslot_of[gp] - 2)] if st else []))
                if cidx == 0 else [])
        for cidx in range(2):
            kt = 2 * p + cidx
            st = (p == 0) and cidx == 0
            sp = (p == P - 1) and cidx == 1
            lastw = sch.op(
                "tensor",
                lambda e, par=par, par_o=par_o, src=src, kt=kt, st=st, sp=sp: e.matmul(
                    O_PS[par_o], a_vb[par][:, kt, :], src[:, (kt % 2) * 512:(kt % 2 + 1) * 512],
                    start=st, stop=sp),
                inc="pe" if cidx == 1 else None)
        lav_val[gp] = lastw
        return lastw

    mask_val = {}
    gslot_of = {}
    lav_queue = []

    for hk in range(HKV):
        par = hk % 2
        ksem = "dkv" + str(par)
        kvg = {}
        for j in range(8):
            w1 = sch.dma("sync", a_kb[par][:, 8 * j:8 * j + 8, :],
                         kg_r[hk // 2][hk % 2][:, :, j * 128:(j + 1) * 128], ksem,
                         waits=[("cck", hk // 2 + 1) if j == 0 else None,
                                ("ccv", j // 2 + 1) if hk == 0 else None,
                                kvbuf_free[par]])
            w2 = sch.dma("sync", a_vb[par][:, 8 * j:8 * j + 8, :],
                         vg_r[j // 2][j % 2][:, :, hk * 128:(hk + 1) * 128], ksem)
            kvg[j] = w2
        kv_full = w2
        for slot in range(8):
            P = 4 * slot + 4
            par_o = gslot % 2
            for p in range(P):
                pbuf = gpair % 2
                PS_pair = PSA if pbuf == 0 else PSB
                gslot_of[gpair] = gslot
                masked = p >= P - 4
                sch.op(
                    "tensor",
                    lambda e, PS_pair=PS_pair, par=par, p=p, hk=hk, slot=slot: e.matmul(
                        PS_pair[:, 0:512], a_kb[par][:, 2 * p, :],
                        b_qT[:, slot, hk * 4:(hk + 1) * 4, :],
                        start=True, stop=True),
                    waits=[exp_val.get(gpair - 2), kvg[p // 4],
                           w_qlast if gpair == 0 else None])
                w_qk = sch.op(
                    "tensor",
                    lambda e, PS_pair=PS_pair, par=par, p=p, hk=hk, slot=slot: e.matmul(
                        PS_pair[:, 512:1024], a_kb[par][:, 2 * p + 1, :],
                        b_qT[:, slot, hk * 4:(hk + 1) * 4, :],
                        start=True, stop=True),
                    inc="pe")
                qk_val[gpair] = w_qk
                if p == 0 and pending is not None:
                    pgs, ppar_o, w_recip = pending
                    w_bc = sch.op(
                        "tensor",
                        lambda e, ppar_o=ppar_o: e.matmul(
                            BC_PS[ppar_o], g_onesrf[:], g_rl[:], start=True, stop=True),
                        waits=[w_recip], inc="pe")
                    bcast_val[pgs] = w_bc
                    w_evb = sch.op("vector", lambda e, ppar_o=ppar_o: e.tensor_copy(
                        g_rlb[:], BC_PS[ppar_o]), waits=[w_bc], inc="dve")
                    phk, pslot = divmod(pgs, 8)
                    w_mu = sch.op(
                        "vector",
                        lambda e, ppar_o=ppar_o, phk=phk, pslot=pslot: e.tensor_tensor(
                            c_oT[:, pslot, phk * 4:(phk + 1) * 4, :],
                            O_PS[ppar_o], g_rlb[:], ALU.mult),
                        waits=[w_evb] + ([stage_state["free"], stage_state["cast_free"]]
                                         if pgs == 0 else []),
                        inc="dve")
                    norm_done[pgs] = w_mu
                    pending = None
                if len(lav_queue) >= 2:
                    emit_lav(*lav_queue.pop(0))
                w_exp = sch.op(
                    "scalar",
                    lambda e, PS_pair=PS_pair, pbuf=pbuf: e.activation(
                        g_pt[pbuf][:], PS_pair[:], AF.Exp, scale=SM_SCALE),
                    waits=[w_qk, lav_val.get(gpair - 2), w_masks if gpair == 0 else None],
                    inc="act")
                exp_val[gpair] = w_exp
                if masked:
                    jj0 = 2 * p - 8 * slot
                    sch.op(
                        "vector",
                        lambda e, pbuf=pbuf, jj0=jj0: e.tensor_tensor(
                            g_ptm[pbuf][:, 0:512], g_pt[pbuf][:, 0:512],
                            t_masks[:, jj0, :], ALU.mult),
                        waits=[w_exp, lav_val.get(gpair - 2)])
                    w_m = sch.op(
                        "vector",
                        lambda e, pbuf=pbuf, jj0=jj0: e.tensor_tensor(
                            g_ptm[pbuf][:, 512:1024], g_pt[pbuf][:, 512:1024],
                            t_masks[:, jj0 + 1, :], ALU.mult),
                        inc="dve")
                    mask_val[gpair] = w_m
                lav_queue.append((gpair, (hk, slot, p, par, par_o, masked, P)))
                gpair += 1
            while lav_queue:
                emit_lav(*lav_queue.pop(0))
            w_recip = sch.op(
                "vector",
                lambda e, par_o=par_o: e.reciprocal(g_rl[:], L_PS[par_o]),
                waits=[lav_val[gpair - 1], bcast_val.get(gslot - 1)], inc="dve")
            pending = (gslot, par_o, w_recip)
            gslot += 1
        kvbuf_free[par] = lav_val[gpair - 1]
    pgs, ppar_o, w_recip = pending
    w_bc = sch.op("tensor",
                  lambda e, ppar_o=ppar_o: e.matmul(BC_PS[ppar_o], g_onesrf[:], g_rl[:],
                                                    start=True, stop=True),
                  waits=[w_recip], inc="pe")
    w_evb = sch.op("vector", lambda e, ppar_o=ppar_o: e.tensor_copy(g_rlb[:], BC_PS[ppar_o]),
                   waits=[w_bc], inc="dve")
    phk, pslot = divmod(pgs, 8)
    w_mu = sch.op(
        "vector",
        lambda e, ppar_o=ppar_o, phk=phk, pslot=pslot: e.tensor_tensor(
            c_oT[:, pslot, phk * 4:(phk + 1) * 4, :],
            O_PS[ppar_o], g_rlb[:], ALU.mult),
        waits=[w_evb], inc="dve")
    norm_done[pgs] = w_mu
    pending = None
    w_attn_done = w_mu
    w_attn_pe = lav_val[gpair - 1]

    wo_r = wo_g.ap().rearrange("(h p) n -> p h n", p=128)
    wo_free = {0: None, 1: None}
    if (gpair - 1) % 2 == 0:
        _psa_last, _psb_last = exp_val[gpair - 1], exp_val[gpair - 2]
    else:
        _psa_last, _psb_last = exp_val[gpair - 2], exp_val[gpair - 1]
    sl_free = [_psa_last, _psa_last, _psb_last, _psb_last]
    xin_free = {0: None, 1: None}
    xrs_free = {0: None, 1: None}
    dxr_val = {}
    sloti = 0
    for ns in range(8):
        par = ns % 2
        osem = "dwo" + str(par)
        sch.dma("sync", a_wo[par][:, 0:16, :],
                wo_r[:, 0:16, ns * 512:(ns + 1) * 512], osem,
                waits=[w_attn_pe if ns < 2 else wo_free[par],
                       ("ccwo", 1) if ns < 2 else None])
        w_wo = sch.dma("sync", a_wo[par][:, 16:32, :],
                       wo_r[:, 16:32, ns * 512:(ns + 1) * 512], osem)
        for tt in range(NT):
            sl = sloti % 4
            par2 = sloti % 2
            sloti += 1
            acc = ACCS[sl]
            w_xi = sch.dma("sync", g_xin[par2][:],
                           x_in[tt * 128:(tt + 1) * 128, ns * 512:(ns + 1) * 512],
                           "dxin" + str(par2),
                           waits=[xin_free[par2],
                                  w_attn_pe if sloti <= 2 else None,
                                  w_attn_done if sloti <= 2 else None])
            last = None
            for h in range(HQ):
                last = sch.op(
                    "tensor",
                    lambda e, acc=acc, par=par, h=h, tt=tt: e.matmul(
                        acc, c_oT[:, tt, h, :], a_wo[par][:, h, :],
                        start=(h == 0), stop=(h == HQ - 1)),
                    waits=[w_wo, sl_free[sl], w_attn_done if sloti == 1 else None]
                    if h == 0 else [],
                    inc="pe" if h == HQ - 1 else None)
            w_add = sch.op(
                "vector",
                lambda e, acc=acc, par2=par2: e.tensor_tensor(
                    g_xrs[par2][:], acc, g_xin[par2][:], ALU.add),
                waits=[last, w_xi, xrs_free[par2]], inc="dve")
            sl_free[sl] = w_add
            xin_free[par2] = w_add
            w_wr = sch.dma("sync",
                           xres_d.ap()[tt * 128:(tt + 1) * 128, ns * 512:(ns + 1) * 512],
                           g_xrs[par2][:], "dxr", waits=[w_add])
            xrs_free[par2] = w_wr
            dxr_val[(ns, tt)] = w_wr
        wo_free[par] = last
    w_outproj_pe = last

    w12_r = w12t_g.ap().rearrange("(r kc p) m -> r p kc m", r=8, p=128)
    w3_r = w3t_g.ap().rearrange("(r mc p) n -> r p mc n", r=8, p=128)
    w12_val = {}
    for q in range(4):
        w12_val[(0, q)] = sch.dma(
            "sync", b_w12[:, q * 8:(q + 1) * 8, :],
            w12_r[0][:, q * 8:(q + 1) * 8, :], "dw12g" + str(q),
            waits=[("ccw12", 1), w_attn_pe])
    w_b3 = sch.dma("gpsimd", t_b3[:], b3.ap().rearrange("(o n) -> o n", o=1), "db3",
                   waits=[w_attn_done])
    h2_pe = {0: None, 1: None}
    evac_free = {0: None, 1: None}
    w_h2T = None
    cxrb = [c_xres, c_xres2]
    for tt in range(NT):
        xb = cxrb[tt % 2]
        pp = tt % 2
        w_xr = sch.dma("sync", xb[:], xres_d.ap()[tt * 128:(tt + 1) * 128, :],
                       "dxb" + str(pp),
                       waits=[("dxr", sch.counts["dxr"]), w_outproj_pe, h2_pe[pp]])
        w_bs = None
        for i in range(8):
            w_bs = sch.op(
                "vector",
                lambda e, i=i, xb=xb, pp=pp: e.bn_stats(g_stats[pp][:, i * 6:(i + 1) * 6],
                                                        xb[:, i * 512:(i + 1) * 512]),
                waits=[w_xr] if i == 0 else [], inc="dve" if i == 7 else None)
        w_ag = sch.op("vector", lambda e, pp=pp: e.bn_aggr(g_mv[pp][:], g_stats[pp][:]),
                      waits=[w_bs], inc="dve")
        w_ln = sch.op("scalar",
                      lambda e, pp=pp: e.activation(g_t1[pp][:], g_mv[pp][:, 1:2],
                                                    AF.Ln, bias=g_eps[:]),
                      waits=[w_ag], inc="act")
        w_rs = sch.op("scalar",
                      lambda e, pp=pp: e.activation(g_rstd[pp][:], g_t1[pp][:],
                                                    AF.Exp, scale=-0.5),
                      waits=[w_ln], inc="act")
        w_ap = sch.op(
            "vector",
            lambda e, xb=xb, pp=pp: e.tensor_scalar(xb[:], xb[:], g_mv[pp][:, 0:1],
                                                    g_rstd[pp][:],
                                                    ALU.subtract, ALU.mult),
            waits=[w_rs], inc="dve")
        for kb in range(4):
            par = kb % 2
            trb = TRB[par]
            last_tr = None
            for kc8 in range(8):
                kc = kb * 8 + kc8
                last_tr = sch.op(
                    "tensor",
                    lambda e, trb=trb, kc8=kc8, kc=kc, xb=xb: e.transpose(
                        trb[:, kc8 * 128:(kc8 + 1) * 128],
                        xb[:, kc * 128:(kc + 1) * 128], g_id[:]),
                    waits=[w_ap, evac_free[par]]
                    + ([w_attn_done, sl_free[0], sl_free[1], sl_free[2], sl_free[3]]
                       if tt == 0 and kb < 2 else [])
                    if kc8 == 0 else [],
                    inc="pe" if kc8 == 7 else None)
            w_ev = sch.op(
                "scalar",
                lambda e, trb=trb, kb=kb, tt=tt: e.copy(
                    a_h2T[:, kb * 8:(kb + 1) * 8, tt * 128:(tt + 1) * 128],
                    trb[:].rearrange("p (a q) -> p a q", a=8)),
                waits=[last_tr], inc="act")
            evac_free[par] = w_ev
        h2_pe[tt % 2] = last_tr
        w_h2T = w_ev

    acc6_free = [None] * 6
    x2s_free = {0: None, 1: None}
    silu_val = {}
    w_w12_pe = {}
    q_done = {}
    sloti = 0
    for r in range(8):
        for q in range(4):
            if (r, q) not in w12_val:
                w12_val[(r, q)] = sch.dma(
                    "sync", b_w12[:, q * 8:(q + 1) * 8, :],
                    w12_r[r][:, q * 8:(q + 1) * 8, :], "dw12g" + str(q),
                    waits=[q_done.get((r - 1, q))])
        for jj in range(8):
            j = r * 8 + jj
            slA = sloti % 6
            slB = (sloti + 1) % 6
            sloti += 2
            accA, accB = ACCS6[slA], ACCS6[slB]
            last = None
            for fc in range(FC):
                qq = fc // 8
                sch.op(
                    "tensor",
                    lambda e, acc=accA, fc=fc, jj=jj: e.matmul(
                        acc, b_w12[:, fc, jj * 128:(jj + 1) * 128],
                        a_h2T[:, fc, 0:512],
                        start=(fc == 0), stop=(fc == FC - 1)),
                    waits=([w12_val[(r, qq)], acc6_free[slA],
                            acc6_free[slB], w_h2T,
                            w_attn_done if sloti <= 6 else None]
                           if fc % 8 == 0 else []))
                last = sch.op(
                    "tensor",
                    lambda e, acc=accB, fc=fc, jj=jj: e.matmul(
                        acc, b_w12[:, fc, jj * 128:(jj + 1) * 128],
                        a_h2T[:, fc, 512:1024],
                        start=(fc == 0), stop=(fc == FC - 1)),
                    inc="pe" if (fc % 8 == 7 and jj == 7) or fc == FC - 1 else None)
                if fc % 8 == 7 and jj == 7:
                    q_done[(r, qq)] = last
            w_w12_pe[r] = last
            for th, acc, sl in ((0, accA, slA), (1, accB, slB)):
                if j < 32:
                    w_si = sch.op(
                        "scalar",
                        lambda e, acc=acc, j=j, th=th: e.activation(
                            c_x4[:, j, th * 512:(th + 1) * 512], acc, AF.Silu,
                            bias=g_b12[:, j:j + 1]),
                        waits=[last, w_b12], inc="act")
                    acc6_free[sl] = w_si
                    silu_val[(j, th)] = w_si
                else:
                    par2 = th
                    w_x2 = sch.op(
                        "scalar",
                        lambda e, acc=acc, j=j, par2=par2: e.activation(
                            g_x2s[par2][:], acc, AF.Identity, bias=g_b12[:, j:j + 1]),
                        waits=[last, x2s_free[par2]], inc="act")
                    acc6_free[sl] = w_x2
                    w_x4 = sch.op(
                        "vector",
                        lambda e, j=j, th=th, par2=par2: e.tensor_tensor(
                            c_x4[:, j - 32, th * 512:(th + 1) * 512],
                            c_x4[:, j - 32, th * 512:(th + 1) * 512],
                            g_x2s[par2][:], ALU.mult),
                        waits=[w_x2, silu_val[(j - 32, th)]], inc="dve")
                    x2s_free[par2] = w_x4
    w_x4_last = w_x4

    w3buf_free = {0: None, 1: None}
    sl_free = [acc6_free[0], acc6_free[1], acc6_free[2], acc6_free[3]]
    xin_free = {0: None, 1: None}
    xrs_free = {0: None, 1: None}
    w_out = None
    sloti = 0
    for ns in range(8):
        par = ns % 2
        w_w3 = sch.dma("sync", b_w3[par][:], w3_r[ns], "dw3g" + str(par),
                       waits=[w_w12_pe[7] if ns < 2 else w3buf_free[par],
                              ("ccw3", 1) if ns < 2 else None])
        for tt in range(NT):
            sl = sloti % 4
            par2 = sloti % 2
            sloti += 1
            acc = ACCS[sl]
            w_xi = sch.dma("sync", g_xin[par2][:],
                           xres_d.ap()[tt * 128:(tt + 1) * 128, ns * 512:(ns + 1) * 512],
                           "dxr3d" + str(par2), waits=[xin_free[par2]])
            last = None
            for mc in range(32):
                last = sch.op(
                    "tensor",
                    lambda e, acc=acc, par=par, mc=mc, tt=tt: e.matmul(
                        acc, c_x4[:, mc, tt * 128:(tt + 1) * 128], b_w3[par][:, mc, :],
                        start=(mc == 0), stop=False),
                    waits=[w_w3, sl_free[sl], w_x4_last if sloti == 1 else None]
                    if mc == 0 else [])
            last = sch.op(
                "tensor",
                lambda e, acc=acc, ns=ns: e.matmul(
                    acc, g_onesr[:], t_b3[0:1, ns * 512:(ns + 1) * 512],
                    start=False, stop=True),
                waits=[w_b3 if sloti == 1 else None], inc="pe")
            w_add = sch.op(
                "vector",
                lambda e, acc=acc, par2=par2: e.tensor_tensor(
                    g_xrs[par2][:], acc, g_xin[par2][:], ALU.add),
                waits=[last, w_xi, xrs_free[par2]], inc="dve")
            sl_free[sl] = w_add
            xin_free[par2] = w_add
            w_out = sch.dma("sync",
                            out[tt * 128:(tt + 1) * 128, ns * 512:(ns + 1) * 512],
                            g_xrs[par2][:], "dout", waits=[w_add])
            xrs_free[par2] = w_out
        w3buf_free[par] = last

    sch.op("sync", lambda e: e.nop(), waits=[w_out])

    with ctx:
        with nc.Block() as block:
            block.tensor(lambda e: sch.replay("tensor", e))
            block.vector(lambda e: sch.replay("vector", e))
            block.scalar(lambda e: sch.replay("scalar", e))
            block.gpsimd(lambda e: sch.replay("gpsimd", e))
            block.sync(lambda e: sch.replay("sync", e))
    return nc


_NC = None


def _get_nc():
    global _NC
    if _NC is None:
        _NC = build_nc()
    return _NC


def _host_tables(c):
    pos = ((np.arange(TLOC) // 128) * NCORE + c) * 128 + (np.arange(TLOC) % 128)
    freqs = 1.0 / (10000.0 ** (np.arange(0, HD, 2, dtype=np.float64) / HD))
    ang = pos[:, None].astype(np.float64) * freqs[None, :]
    cos = np.cos(ang).astype(np.float32)
    sin = np.sin(ang).astype(np.float32)
    Cs = np.empty((128, TLOC), dtype=np.float32)
    Ss = np.empty((128, TLOC), dtype=np.float32)
    Cs[0::2, :] = cos.T
    Cs[1::2, :] = cos.T
    Ss[0::2, :] = -sin.T
    Ss[1::2, :] = sin.T
    tri = np.triu(np.ones((128, 128), dtype=np.float32))
    m = np.empty((8, 128, 512), dtype=ml_dtypes.bfloat16)
    for jj in range(8):
        if jj < c:
            blk = np.ones((128, 128), dtype=np.float32)
        elif jj == c:
            blk = tri
        else:
            blk = np.zeros((128, 128), dtype=np.float32)
        m[jj] = np.tile(blk, (1, 4)).astype(ml_dtypes.bfloat16)
    return Cs, Ss, m


def kernel(L0_attn_x, L0_attn_wq, L0_attn_wk, L0_attn_wv, L0_attn_wo,
           L0_ffn_w12, L0_ffn_b12, L0_ffn_w3, L0_ffn_b3):
    x = np.ascontiguousarray(np.asarray(L0_attn_x, dtype=np.float32).reshape(8192, D))
    wq = np.asarray(L0_attn_wq, dtype=np.float32)
    wk = np.asarray(L0_attn_wk, dtype=np.float32)
    wv = np.asarray(L0_attn_wv, dtype=np.float32)
    wo = np.asarray(L0_attn_wo, dtype=np.float32)
    w12 = np.asarray(L0_ffn_w12, dtype=np.float32)
    b12 = np.ascontiguousarray(np.asarray(L0_ffn_b12, dtype=np.float32))
    w3 = np.asarray(L0_ffn_w3, dtype=np.float32)
    b3 = np.ascontiguousarray(np.asarray(L0_ffn_b3, dtype=np.float32))

    xt = x.reshape(64, 128, D)
    in_maps = []
    for c in range(NCORE):
        Cs, Ss, m = _host_tables(c)
        in_maps.append({
            "x": np.ascontiguousarray(xt[c::NCORE].reshape(TLOC, D)),
            "wqs": np.ascontiguousarray(wq[c * 512:(c + 1) * 512]),
            "wks": np.ascontiguousarray(wk[c * 512:(c + 1) * 512]),
            "wvs": np.ascontiguousarray(wv[c * 512:(c + 1) * 512]),
            "wos": np.ascontiguousarray(wo[c * 512:(c + 1) * 512]),
            "w12s": np.ascontiguousarray(w12[c * 1024:(c + 1) * 1024]),
            "w3s": np.ascontiguousarray(w3[c * 512:(c + 1) * 512]),
            "b12": b12, "b3": b3,
            "ropec": Cs, "ropes": Ss, "masks": m,
        })

    nc = _get_nc()
    res = run_bass_kernel_spmd(nc, in_maps, core_ids=list(range(NCORE)))
    full = np.empty((64, 128, D), dtype=np.float32)
    for c in range(NCORE):
        full[c::NCORE] = res.results[c]["out"].reshape(NT, 128, D)
    return full.reshape(1, 8192, D)
